# version 09
# baseline (speedup 1.0000x reference)
import math
import numpy as np
import ml_dtypes
import concourse.bass as bass
import concourse.mybir as mybir
from concourse.bass_utils import run_bass_kernel_spmd

F32 = mybir.dt.float32
BF16 = mybir.dt.bfloat16
AF = mybir.ActivationFunctionType
ALU = mybir.AluOpType
AX = mybir.AxisListType
bf16 = ml_dtypes.bfloat16

B, S, D = 2, 4096, 2048
H, DH, RD = 16, 128, 64
KVD, QLD = 512, 1536
T, NT, DC = 1024, 8, 16
SA_A = 1.0 / math.sqrt(DH + RD)
SA_B = 1.0 / math.sqrt(math.sqrt(DH))
EPS = 1e-5
NSUB = 4
DEBUG = False

KRA = D + RD + 1
KRB = D + 1

ENGINES = ("sync", "scalar", "vector", "tensor", "gpsimd")
GROUPS = [[0, 1, 2, 3], [4, 5, 6, 7]]


def ts(i, w=128):
    return slice(i * w, (i + 1) * w)


class MK:
    def __init__(self, nsub=NSUB):
        self.nsub = nsub
        self.nc = bass.Bass("TRN2", target_bir_lowering=False)
        self.streams = {e: [] for e in ENGINES}
        self.cnt = {}
        self.sems = {}
        self._regpar = 0
        self.ctb_free = 0
        self.pending_xln = None
        self._declare_io()
        self._alloc()

    def _sem(self, name):
        if name not in self.sems:
            self.sems[name] = self.nc.alloc_semaphore(name=name)
            self.cnt[name] = 0
        return self.sems[name]

    def n(self, name):
        self._sem(name)
        return self.cnt[name]

    def op(self, eng, fn, waits=(), inc=None, by=1):
        waits = [(sn, v) for (sn, v) in waits if sn is not None and v and v > 0]
        for sn, _ in waits:
            self._sem(sn)
        if inc is not None:
            self._sem(inc)
            self.cnt[inc] += by
        self.streams[eng].append((fn, waits, inc, by))
        return self.cnt[inc] if inc is not None else None

    def dma(self, eng, out, in_, waits=(), inc=None, **kw):
        return self.op(eng, lambda q, o=out, i=in_, k=dict(kw): q.dma_start(out=o, in_=i, **k),
                       list(waits), inc, by=16)

    def _declare_io(self):
        nc = self.nc

        def inp(name, shape, dt=F32):
            return nc.dram_tensor(name, list(shape), dt, kind="ExternalInput").ap()

        self.x_in = inp("x", (T, D))
        self.wa = {k: inp("a_" + k, sh) for k, sh in {
            "w_dkv": (D, KVD), "b_dkv": (1, KVD), "w_dq": (D, QLD), "b_dq": (1, QLD),
            "w_rk": (D, RD), "b_rk": (1, RD), "w_rq": (QLD, H * RD), "b_rq": (1, H * RD),
            "w_uk": (KVD, D), "b_uk": (1, D), "w_uv": (KVD, D), "b_uv": (1, D),
            "w_uq": (QLD, D), "b_uq": (1, D), "w_o": (D, D), "b_o": (1, D)}.items()}
        self.wbq = {k: inp("b_" + k, sh) for k, sh in {
            "Wq": (D, D), "bq": (1, D), "Wk": (D, D), "bk": (1, D),
            "Wv": (D, D), "bv": (1, D), "Wo": (D, D), "bo": (1, D)}.items()}
        self.wf = {k: inp("f_" + k, sh) for k, sh in
                   {"w": (D, D), "s": (1, D), "b": (1, D)}.items()}
        self.cos_in = inp("cos_tab", (128, T))
        self.sin_in = inp("sin_tab", (128, T))
        self.msk_in = inp("masks", (4 * 128, 128))
        self.out_d = nc.dram_tensor("out", [T, D], F32, kind="ExternalOutput").ap()

        def scr(name, shape):
            return nc.dram_tensor(name, list(shape), BF16, kind="Internal").ap()

        self.wsc = {k: scr("wsc_" + k, (D, D)) for k in ("q", "k", "v", "o", "f")}
        self.wsa = {k: scr("wsa_" + k, sh) for k, sh in {
            "w_dkv": (D, KVD), "w_rk": (D, RD), "w_uk": (KVD, D),
            "w_uv": (KVD, D), "w_dq": (D, QLD), "w_rq": (QLD, H * RD),
            "w_uq": (QLD, D), "w_o": (D, D)}.items()}
        self.rbf_src = {
            "b_dkv": (self.wa["b_dkv"], KVD), "b_dq": (self.wa["b_dq"], QLD),
            "b_rk": (self.wa["b_rk"], RD), "b_rq": (self.wa["b_rq"], H * RD),
            "b_uk": (self.wa["b_uk"], D), "b_uv": (self.wa["b_uv"], D),
            "b_uq": (self.wa["b_uq"], D), "b_o": (self.wa["b_o"], D),
            "bq": (self.wbq["bq"], D), "bk": (self.wbq["bk"], D),
            "bv": (self.wbq["bv"], D), "bo": (self.wbq["bo"], D),
            "f_s": (self.wf["s"], D), "f_b": (self.wf["b"], D)}
        self.rbf = {k: scr("rbf_" + k, (1, n))
                    for k, (_, n) in self.rbf_src.items()}
        for ab in ("a", "b"):
            setattr(self, "ck_" + ab,
                    [scr("ck%s_c%d" % (ab, g), (512, T)) for g in range(4)])
            setattr(self, "gk_" + ab,
                    [scr("gk%s_c%d" % (ab, g), (2048, T)) for g in range(4)])
            setattr(self, "cv_" + ab,
                    [scr("cv%s_c%d" % (ab, g), (512, T)) for g in range(4)])
            setattr(self, "gv_" + ab,
                    [scr("gv%s_c%d" % (ab, g), (2048, T)) for g in range(4)])
            setattr(self, "ckn_" + ab, scr("ckn_" + ab, (1, T)))
            setattr(self, "gkn_" + ab, scr("gkn_" + ab, (4, T)))
        self.ckr_a = scr("ckr_a", (RD, T))
        self.gkr_a = scr("gkr_a", (4 * RD, T))
        self.qsp = scr("qsp", (D + H * RD, T))

        self.ident_c = nc.inline_tensor(np.eye(128, dtype=bf16), "c_ident").ap()
        self.ones_r32_c = nc.inline_tensor(np.ones((1, 128), np.float32), "c_or32").ap()
        self.ones_rbf_c = nc.inline_tensor(np.ones((1, 128), bf16), "c_orbf").ap()
        self.ones_cbf_c = nc.inline_tensor(np.ones((128, 1), bf16), "c_ocbf").ap()
        self.one11_c = nc.inline_tensor(np.ones((1, 1), np.float32), "c_o11").ap()
        self.one11b_c = nc.inline_tensor(np.ones((1, 1), bf16), "c_o11b").ap()

    def _alloc(self):
        sb = self.nc.alloc_sbuf_tensor
        self.R = sb("R", [128, NT * D], F32)
        self.SAb = sb("SAb", [128, 16384], BF16)
        self.WB = sb("WB", [128, 16384], BF16)
        self.AT = sb("AT", [128, 16384], BF16)
        self.SC2 = sb("SC2", [128, 2048], BF16)
        self.SQB = sb("SQB", [128, 1024], BF16)
        self.LNS = sb("LNS", [128, 80], F32)
        self.BCOL = sb("BCOL", [128, 16], F32)
        self.ROW = sb("ROW", [1, 4096], BF16)
        self.XLNW = self.SC2.ap()
        self.MSK = sb("MSK", [128, 512], BF16)
        self.RKG = sb("RKG", [64, 4096], BF16)
        self.STG = sb("STG", [128, 2048], BF16)
        self.RQB = sb("RQB", [64, 2048], BF16)
        self.PBuf = sb("PBuf", [128, 2048], BF16)
        self.NRM = sb("NRM", [1, 384], F32)
        self.KROW = sb("KROW", [1, 1024], BF16)
        self.NEGM = sb("NEGM", [128, 32], F32)
        self.RLB = sb("RLB", [128, 512], F32)
        self.RLr = self.RLB.ap()[0:1, :]
        self.NRMB = sb("NRMB", [1, 128], BF16)
        self.IDN = sb("IDN", [128, 128], BF16)
        self.ONER32 = sb("ONER32", [1, 128], F32)
        self.ONERBF = sb("ONERBF", [1, 128], BF16)
        self.ONECBF = sb("ONECBF", [128, 1], BF16)
        self.ONE11 = sb("ONE11", [1, 1], F32)
        self.ONE11B = sb("ONE11B", [1, 1], BF16)
        self.CTB = sb("CTB", [128, 1024], F32)
        self.SSLB = sb("SSLB", [128, 512], BF16)
        self.TMPB = self.SQB.ap()[:, 0:1024].bitcast(F32)

        pa = self.nc.alloc_psum_tensor
        self.PS_A = pa("PS_A", [128, 1024], F32).ap()
        self.PS_B = pa("PS_B", [128, 1024], F32).ap()
        self.PO_A = pa("PO_A", [128, 512], F32).ap()
        self.PO_B = pa("PO_B", [128, 512], F32).ap()
        self.PLp = pa("PLp", [1, 512], F32).ap()
        self.PXp = pa("PXp", [128, 512], F32).ap()
        self.PXb = self.PXp.bitcast(BF16)
        self.PSLOT = [self.PS_A[:, 0:512], self.PS_A[:, 512:1024],
                      self.PS_B[:, 0:512], self.PS_B[:, 512:1024]]
        self.PPAIR = [self.PS_A, self.PS_B]
        self.PO = [self.PO_A, self.PO_B]

        self._rowpar = 0
        self.row_free = {0: (None, 0), 1: (None, 0)}
        self.row_cur = {}

    def load_row(self, name, src_ap, n):
        slot = self._rowpar % 2
        self._rowpar += 1
        sem = "gwR%d" % slot
        w = [self.row_free[slot]] if self.row_free[slot][0] else []
        gw = self.dma("sync", self.ROW[0:1, slot * 2048: slot * 2048 + n],
                      self.rbf[name][0:1, 0:n],
                      waits=w + [("gwB", self.gwb_all)], inc=sem)
        self.row_cur[name] = (slot, gw)

        def slicer(lo=0, nn=None, slot=slot, n=n):
            nn = n - lo if nn is None else nn
            return self.ROW[0:1, slot * 2048 + lo: slot * 2048 + lo + nn]
        return slicer, sem, gw

    def row_done(self, name, sem, cnt):
        slot, _ = self.row_cur[name]
        self.row_free[slot] = (sem, cnt)

    def xlnt(self, k, c0, nn):
        return self.SAb[:, k * 1024 + c0: k * 1024 + c0 + nn]

    def emit_init(self):
        s = self
        for c in range(4):
            s.dma("sync", s.R[:, c * 2 * D:(c + 1) * 2 * D].rearrange(
                "p (t d) -> p t d", d=D),
                s.x_in[c * 256:(c + 1) * 256, :].rearrange(
                    "(t p) d -> p t d", p=128), inc="hsX%d" % c)
        s.dma("gpsimd", s.MSK[:, :].rearrange("p (r q) -> p r q", q=128),
              s.msk_in.rearrange("(r p) q -> p r q", p=128), inc="gwI")
        s.dma("sync", s.IDN[:, :], s.ident_c, inc="hs")
        s.dma("sync", s.ONER32[:, :], s.ones_r32_c, inc="hs")
        s.dma("sync", s.ONERBF[:, :], s.ones_rbf_c, inc="hs")
        s.dma("sync", s.ONECBF[:, :], s.ones_cbf_c, inc="hs")
        s.dma("sync", s.ONE11[:, :], s.one11_c, inc="hs")
        s.dma("sync", s.ONE11B[:, :], s.one11b_c, inc="hs")
        self.init_hs = s.n("hs")
        self.msk_gw = s.n("gwI")
        s.op("vector", lambda q: q.memset(s.KROW[0:1, :], 0.0), inc="ve")

        self.pc = {}
        self.px_free = s.n("ve")
        self.stg_free = [(None, 0)] * 2
        self.pl_free = s.n("ve")
        for nm, (src, n) in self.rbf_src.items():
            s.dma("gpsimd", s.rbf[nm], src[0:1, 0:n], inc="gwB")
        self.gwb_all = s.n("gwB")
        self.ca = {}
        for nm in ("w_dkv", "w_rk", "w_uk", "w_uv", "w_dq", "w_rq",
                   "w_uq"):
            s.dma("gpsimd", s.wsa[nm], s.wa[nm],
                  waits=[("gwC", s.n("gwC"))], inc="gwC")
            self.ca[nm] = ("gwC", s.n("gwC"))

    def emit_casts(self):
        s = self
        for nm, src2 in [("f", s.wf["w"]), ("q", s.wbq["Wq"]),
                         ("k", s.wbq["Wk"]), ("v", s.wbq["Wv"]),
                         ("o", s.wbq["Wo"])]:
            s.dma("gpsimd", s.wsc[nm], src2, inc="gwP")
        pc_all = s.n("gwP")
        for nm in ("f", "q", "k", "v", "o"):
            self.pc[nm] = pc_all

    def bcol(self, name, src_ap, nchunks, scale=None):
        s = self
        rower, rsem, gw = s.load_row(name, src_ap, nchunks * 128)
        pe = None
        guard = s.n("se")
        for c in range(nchunks):
            pe = s.op("tensor", lambda q, c=c, rower=rower: q.matmul(
                s.PXp[:, c:c + 1], rower(c * 128, 128), s.ONE11B[:, :],
                start=True, stop=True),
                waits=[("hs", self.init_hs), (rsem, gw),
                       ("ve", self.px_free)], inc="pe")
        s.row_done(name, "pe", s.n("pe"))
        ve = s.op("vector", lambda q, nn=nchunks: q.tensor_copy(
            s.BCOL[:, 0:nn], s.PXp[:, 0:nn]),
            waits=[("pe", pe), ("se", guard)], inc="ve")
        if scale is not None:
            ve = s.op("vector", lambda q, nn=nchunks, sc=scale: q.tensor_scalar(
                s.BCOL[:, 0:nn], s.BCOL[:, 0:nn], sc, None, ALU.mult),
                waits=[("ve", ve)], inc="ve")
        self.px_free = ve
        return ve

    class WStream:

        def __init__(self, mk, loader, nslices, post=None):
            self.mk, self.loader, self.nslices = mk, loader, nslices
            self.post = post
            self.ld = {}

        def prefetch(self, si):
            if si >= self.nslices or si in self.ld:
                return
            mk = self.mk
            reg = 8192 * (mk._regpar % 2)
            mk._regpar += 1
            ld = self.loader(reg, si)
            if self.post is not None:
                ld = self.post(reg, si, ld)
            self.ld[si] = (reg, ld)

        def get(self, si):
            self.prefetch(si)
            return self.ld[si]

    def wstream_straight(self, src, nchunks, sw, nslices, cast_w=()):
        qsz = max(1, nchunks // 4)
        nq = (nchunks + qsz - 1) // qsz

        def loader(reg, si):
            pew = self.n("pe")
            cnts = []
            for qq in range(nq):
                sem = "gwW%d_%d" % (reg // 8192, qq)
                k0 = qq * qsz
                k1 = min(nchunks, k0 + qsz)
                ap = src[k0 * 128:k1 * 128, si * sw:(si + 1) * sw].rearrange(
                    "(k p) c -> p k c", p=128)
                cnts.append((sem, self.dma(
                    "sync",
                    self.WB[:, reg + k0 * sw:reg + k1 * sw].rearrange(
                        "p (k c) -> p k c", c=sw),
                    ap, waits=[("pe", pew)] + list(cast_w), inc=sem)))
            return (None, cnts, qsz)
        return MK.WStream(self, loader, nslices)

    @staticmethod
    def qgate(ld, k):
        if isinstance(ld[1], list):
            return ld[1][min(k // ld[2], len(ld[1]) - 1)]
        return (ld[0], ld[1])

    @staticmethod
    def qall(ld):
        if isinstance(ld[1], list):
            return list(ld[1])
        return [(ld[0], ld[1])]

    def wstream_T(self, src_bf, sw, nslices, pc_gw, post=None):
        def loader(reg, si):
            sem = "haW%d" % (reg // 8192)
            hac = None
            pew = self.n("pe")
            for i in range(16):
                hac = self.dma("sync", self.WB[:, reg + i * sw: reg + (i + 1) * sw],
                               src_bf[si * sw:(si + 1) * sw, ts(i)],
                               waits=[("pe", pew), ("gwP", pc_gw)], inc=sem,
                               transpose=True)
            return (sem, hac)
        return MK.WStream(self, loader, nslices, post=post)

    def ln_begin(self):
        self._ln_tp = self.n("pe")

    def emit_ln_tile(self, t, waits0, gate_pe):
        s = self
        slots = {"S1": 0, "S2a": 8, "S2b": 16, "MEAN": 24, "MSQ": 32, "VAR": 40,
                 "LNV": 48, "RSTD": 56, "NMR": 64}

        def L(nm, t):
            return s.LNS[:, slots[nm] + t:slots[nm] + t + 1]

        if True:
            Rt = s.R[:, t * D:(t + 1) * D]
            s.op("vector", lambda q, Rt=Rt, t=t: q.tensor_reduce(
                L("S1", t), Rt, AX.X, ALU.add),
                waits=list(waits0), inc="ve")
            s.op("vector", lambda q, Rt=Rt, t=t: q.scalar_tensor_tensor(
                s.SQB[:, 0:1024], Rt[:, 0:1024], 1.0, Rt[:, 0:1024],
                ALU.mult, ALU.mult, accum_out=L("S2a", t)), inc="ve")
            s.op("vector", lambda q, Rt=Rt, t=t: q.scalar_tensor_tensor(
                s.SQB[:, 0:1024], Rt[:, 1024:2048], 1.0, Rt[:, 1024:2048],
                ALU.mult, ALU.mult, accum_out=L("S2b", t)), inc="ve")
            s.op("vector", lambda q, t=t: q.tensor_tensor(
                L("S2a", t), L("S2a", t), L("S2b", t), ALU.add), inc="ve")
            s.op("vector", lambda q, t=t: q.tensor_scalar(
                L("MEAN", t), L("S1", t), 1.0 / D, None, ALU.mult), inc="ve")
            s.op("vector", lambda q, t=t: q.tensor_scalar(
                L("VAR", t), L("S2a", t), 1.0 / D, EPS, ALU.mult, ALU.add), inc="ve")
            s.op("vector", lambda q, t=t: q.tensor_tensor(
                L("MSQ", t), L("MEAN", t), L("MEAN", t), ALU.mult), inc="ve")
            v6 = s.op("vector", lambda q, t=t: q.tensor_tensor(
                L("VAR", t), L("VAR", t), L("MSQ", t), ALU.subtract), inc="ve")
            s.op("scalar", lambda q, t=t: q.activation(
                L("LNV", t), L("VAR", t), AF.Ln), waits=[("ve", v6)], inc="se")
            s2 = s.op("scalar", lambda q, t=t: q.activation(
                L("RSTD", t), L("LNV", t), AF.Exp, scale=-0.5), inc="se")
            v7 = s.op("vector", lambda q, t=t: q.scalar_tensor_tensor(
                L("NMR", t), L("MEAN", t), -1.0, L("RSTD", t), ALU.mult, ALU.mult),
                waits=[("se", s2)], inc="ve")
            sl = 0
            v8 = s.op("vector", lambda q, Rt=Rt, t=t: q.tensor_scalar(
                s.XLNW[:, 0:2048], Rt, L("RSTD", t), L("NMR", t),
                ALU.mult, ALU.add),
                waits=[("ve", v7), ("pe", self._ln_tp)], inc="ve")
            pe_r = None
            for rnd in range(2):
                for j in range(8):
                    k = rnd * 8 + j
                    pe_r = s.op("tensor", lambda q, k=k, j=j: q.transpose(
                        s.PXb[:, ts(j)],
                        s.XLNW[:, k * 128:(k + 1) * 128],
                        s.IDN[:, :]),
                        waits=[("ve", v8), ("ve", self.px_free),
                               ("pe", gate_pe), ("hs", self.init_hs)], inc="pe")
                self.px_free = s.op("vector", lambda q, t=t, rnd=rnd: q.tensor_copy(
                    s.SAb[:, :].rearrange("p (k c) -> p k c", c=1024)[
                        :, rnd * 8:(rnd + 1) * 8, t * 128:(t + 1) * 128],
                    s.PXb[:, 0:1024].rearrange("p (j c) -> p j c", c=128)),
                    waits=[("pe", pe_r)], inc="ve")
            self._ln_tp = pe_r
        return s.n("ve")

    def emit_ln(self, r_ve, sa_free_pe=0, tile_w=None):
        self.ln_begin()
        for t in range(NT):
            tw = tile_w(t) if tile_w else [("hs", self.init_hs)]
            self.emit_ln_tile(t, [("ve", r_ve)] + tw, sa_free_pe)
        return self.n("ve")

    def proj_feat(self, nck, stream, sw, nslices, rhs_fn, epi_fn, extra_w=(),
                  init_free=None):
        s = self
        stream.prefetch(0)
        stream.prefetch(1)
        free = list(init_free) if init_free else [("se", s.n("se"))] * 4
        idx = 0
        mps = sw // 128
        for si in range(nslices):
            reg, ld = stream.get(si)
            for ml in range(mps):
                for nn2 in range(2):
                    slp = idx % 4
                    idx += 1
                    pe = None
                    for k in range(nck):
                        w = []
                        if k == 0:
                            w = [free[slp],
                                 ("hs", self.init_hs)] + list(extra_w)
                        if k == 0 or (isinstance(ld[1], list) and
                                      k % ld[2] == 0):
                            w.append(MK.qgate(ld, k))
                        pe = s.op("tensor", lambda q, reg=reg, k=k, ml=ml, nn2=nn2,
                                  slp=slp, fst=(k == 0), lst=(k == nck - 1), sw=sw:
                                  q.matmul(s.PSLOT[slp],
                                           s.WB[:, reg + k * sw + ml * 128:
                                                reg + k * sw + (ml + 1) * 128],
                                           rhs_fn(k, nn2), start=fst, stop=lst),
                                  waits=w, inc="pe" if k == nck - 1 else None)
                    free[slp] = ("se", epi_fn(si * mps + ml, nn2, slp, pe))
            stream.prefetch(si + 2)

    def proj_tok(self, nck, stream, sw, nslices, lhs_fn, rowfn, row_w, epi_fn,
                 extra_w=()):
        s = self
        stream.prefetch(0)
        stream.prefetch(1)
        free = [(None, 0)] * 4
        idx = 0
        for si in range(nslices):
            reg, ld = stream.get(si)
            for t in range(NT):
                slp = idx % 4
                idx += 1
                fw = [free[slp]] if free[slp][0] else []
                s.op("tensor", lambda q, slp=slp, rowfn=rowfn, si=si:
                     q.matmul(s.PSLOT[slp], s.ONERBF[:, :],
                              rowfn(si * 512, 512), start=True, stop=False),
                     waits=fw + [MK.qgate(ld, 0)] + list(row_w) +
                     [("hs", self.init_hs)] + list(extra_w), inc=None)
                pe = None
                for k in range(nck):
                    w = []
                    if isinstance(ld[1], list) and k % ld[2] == 0 and k > 0:
                        w = [MK.qgate(ld, k)]
                    pe = s.op("tensor", lambda q, reg=reg, k=k, t=t, slp=slp, sw=sw,
                              lst=(k == nck - 1):
                              q.matmul(s.PSLOT[slp], lhs_fn(k, t),
                                       s.WB[:, reg + k * sw: reg + (k + 1) * sw],
                                       start=False, stop=lst),
                              waits=w, inc="pe" if k == nck - 1 else None)
                free[slp] = epi_fn(si, t, slp, pe)
            stream.prefetch(si + 2)

    def emit_out_proj(self, lhs_fn, stream, rowfn, row_w, extra_w=(),
                      ln_next=False):
        s = self
        stream.prefetch(0)
        stream.prefetch(1)
        free = [("ve", 0)] * 4
        idx = 0
        rve = s.n("ve")
        for si in range(4):
            reg, ld = stream.get(si)
            for t in range(NT):
                slp = idx % 4
                idx += 1
                fw = [free[slp]] if free[slp][1] else []
                s.op("tensor", lambda q, slp=slp, si=si, rowfn=rowfn: q.matmul(
                    s.PSLOT[slp], s.ONERBF[:, :], rowfn(si * 512, 512),
                    start=True, stop=False),
                    waits=fw + [MK.qgate(ld, 0)] + list(row_w)
                    + list(extra_w), inc=None)
                pe = None
                for k in range(DC):
                    w = []
                    if isinstance(ld[1], list) and k % ld[2] == 0 and k > 0:
                        w = [MK.qgate(ld, k)]
                    pe = s.op("tensor", lambda q, reg=reg, k=k, t=t, slp=slp,
                              lst=(k == DC - 1):
                              q.matmul(s.PSLOT[slp], lhs_fn(k, t),
                                       s.WB[:, reg + k * 512: reg + (k + 1) * 512],
                                       start=False, stop=lst),
                              waits=w, inc="pe" if k == DC - 1 else None)
                rve = s.op("vector", lambda q, t=t, si=si, slp=slp:
                           q.tensor_tensor(
                               s.R[:, t * D + si * 512: t * D + (si + 1) * 512],
                               s.R[:, t * D + si * 512: t * D + (si + 1) * 512],
                               s.PSLOT[slp], ALU.add),
                           waits=[("pe", pe)], inc="ve")
                free[slp] = ("ve", rve)
                if ln_next and si == 3:
                    if t == 0:
                        self.ln_begin()
                    self.emit_ln_tile(t, [("ve", rve)], s.n("pe"))
            stream.prefetch(si + 2)
        if ln_next:
            self.pending_xln = s.n("ve")
        return rve

    def head_norm(self, src_fn, out_col, kdim=128, extra_w=()):
        s = self
        ones = s.ONECBF[:, :] if kdim == 128 else s.ONECBF[0:kdim, :]
        mx = None
        for g in range(2):
            pe = s.op("tensor", lambda q, g=g, ones=ones: q.matmul(
                s.PLp[0:1, :], ones, src_fn(g), start=True, stop=True),
                waits=[("ve", self.pl_free)] + list(extra_w), inc="pe")
            mx = s.op("vector", lambda q, g=g: q.tensor_reduce(
                s.NRM[0:1, 340 + g:341 + g], s.PLp[0:1, :], AX.X, ALU.max),
                waits=[("pe", pe)], inc="ve")
            self.pl_free = mx
        return s.op("vector", lambda q, oc=out_col: q.tensor_tensor(
            s.NRM[0:1, oc:oc + 1], s.NRM[0:1, 340:341], s.NRM[0:1, 341:342],
            ALU.max), waits=[("ve", mx)], inc="ve")

    def emit_negm(self, gn, is_a, cc_w, qn_ve):
        s = self
        gw = None
        for r in range(4):
            gw = s.dma("scalar", s.NRMB[0:1, r * 32:(r + 1) * 32],
                       gn[r:r + 1, 0:32],
                       waits=list(cc_w), inc="gwN")
        cv = s.op("vector", lambda q: q.tensor_copy(
            s.NRM[0:1, 192:320], s.NRMB[0:1, :]),
            waits=[("gwN", gw)], inc="ve")
        ncol = 17 if is_a else 16
        v = s.op("vector", lambda q, n=ncol: q.tensor_tensor(
            s.NRM[0:1, 136:136 + n], s.NRM[0:1, 192:192 + n],
            s.NRM[0:1, 224:224 + n], ALU.max),
            waits=[("ve", cv), ("ve", qn_ve)], inc="ve")
        v = s.op("vector", lambda q, n=ncol: q.tensor_tensor(
            s.NRM[0:1, 136:136 + n], s.NRM[0:1, 136:136 + n],
            s.NRM[0:1, 256:256 + n], ALU.max), waits=[("ve", v)], inc="ve")
        v = s.op("vector", lambda q, n=ncol: q.tensor_tensor(
            s.NRM[0:1, 136:136 + n], s.NRM[0:1, 136:136 + n],
            s.NRM[0:1, 288:288 + n], ALU.max), waits=[("ve", v)], inc="ve")
        sv = s.op("scalar", lambda q, n=ncol: q.activation(
            s.NRM[0:1, 136:136 + n], s.NRM[0:1, 136:136 + n], AF.Ln),
            waits=[("ve", v)], inc="se")
        sv = s.op("scalar", lambda q, n=ncol: q.activation(
            s.NRM[0:1, 136:136 + n], s.NRM[0:1, 136:136 + n], AF.Exp, scale=0.5),
            waits=[("se", sv)], inc="se")
        qv = s.op("scalar", lambda q: q.activation(
            s.NRM[0:1, 64:96], s.NRM[0:1, 64:96], AF.Ln),
            waits=[("ve", qn_ve)], inc="se")
        qv = s.op("scalar", lambda q: q.activation(
            s.NRM[0:1, 64:96], s.NRM[0:1, 64:96], AF.Exp, scale=0.5),
            waits=[("se", qv)], inc="se")
        if is_a:
            rv = s.op("scalar", lambda q: q.activation(
                s.NRM[0:1, 34:35], s.NRM[0:1, 34:35], AF.Ln),
                waits=[("ve", qn_ve)], inc="se")
            rv = s.op("scalar", lambda q: q.activation(
                s.NRM[0:1, 34:35], s.NRM[0:1, 34:35], AF.Exp, scale=0.5),
                waits=[("se", rv)], inc="se")
            v = s.op("vector", lambda q: q.tensor_scalar(
                s.NRM[0:1, 136:152], s.NRM[0:1, 136:152], s.NRM[0:1, 152:153],
                None, ALU.add),
                waits=[("se", sv), ("se", rv), ("se", qv)], inc="ve")
            v = s.op("vector", lambda q: q.tensor_scalar(
                s.NRM[0:1, 64:96], s.NRM[0:1, 64:96], s.NRM[0:1, 34:35],
                None, ALU.add), waits=[("ve", v)], inc="ve")
        else:
            v = s.op("vector", lambda q: q.tensor_copy(
                s.NRM[0:1, 168:169], s.NRM[0:1, 136:137]),
                waits=[("se", sv), ("se", qv)], inc="ve")
        for g in range(2):
            v = s.op("vector", lambda q, g=g: q.tensor_tensor(
                s.NRM[0:1, 96 + g * 16:112 + g * 16],
                s.NRM[0:1, 64 + g * 16:80 + g * 16],
                s.NRM[0:1, 136:152], ALU.mult), waits=[("ve", v)], inc="ve")
        v = s.op("vector", lambda q: q.tensor_scalar(
            s.NRM[0:1, 96:128], s.NRM[0:1, 96:128], -1.05, None, ALU.mult),
            waits=[("ve", v)], inc="ve")
        pe = s.op("tensor", lambda q: q.matmul(
            s.PXp[:, 0:32], s.ONER32[:, :], s.NRM[0:1, 96:128], start=True,
            stop=True), waits=[("ve", v), ("ve", self.px_free)], inc="pe")
        self.px_free = s.op("vector", lambda q: q.tensor_copy(
            s.NEGM[:, :], s.PXp[:, 0:32]), waits=[("pe", pe)], inc="ve")
        return s.op("vector", lambda q: q.memset(s.NRM[0:1, 342:343], 0.0),
            waits=[("ve", self.px_free)], inc="ngm")

    def emit_attention(self, is_a, gkc, gvc, kw, vw, rw, ngm,
                       sa_free_pe, qsp_w):
        s = self
        kbuf = [s.SAb[:, 0:4096], s.SAb[:, 4096:8192]]
        vbuf = [s.SAb[:, 8192:12288], s.SAb[:, 12288:16384]]
        rkg_hs = 0
        if is_a:
            for r in range(4):
                rkg_hs = s.dma("sync", s.RKG[0:RD, r * 1024:(r + 1) * 1024],
                               s.gkr_a[r * RD:(r + 1) * RD, :],
                               waits=list(rw) + [("pe", sa_free_pe)], inc="hsG")
        k_ld = [0, 0]
        v_ld = [0, 0]
        q_ld = [0, 0]
        slot_pe = [sa_free_pe, sa_free_pe]
        pb_free = [s.n("pe"), s.n("pe")]
        ps_free = [s.n("se"), s.n("se")]
        po_free = [s.n("ve"), s.n("ve")]
        attnt_pe = sa_free_pe
        gpair = [0]
        pending = [None]

        def load_head(h):
            sl = h % 2
            hg, hh = h // 4, h % 4
            for r in range(4):
                k_ld[sl] = s.dma("sync", kbuf[sl][:, r * 1024:(r + 1) * 1024],
                                 gkc[hg][r * 512 + hh * 128:
                                         r * 512 + (hh + 1) * 128, :],
                                 waits=[kw[hg], ("pe", slot_pe[sl])],
                                 inc="hsA%d" % sl)
                v_ld[sl] = s.dma("scalar",
                                 vbuf[sl][:, r * 1024:(r + 1) * 1024],
                                 gvc[hg][r * 512 + hh * 128:
                                         r * 512 + (hh + 1) * 128, :],
                                 waits=[vw[hg], ("pe", slot_pe[sl])],
                                 inc="haA%d" % sl)
            q_ld[sl] = s.dma("scalar", s.AT[:, h * 1024:(h + 1) * 1024],
                             s.qsp[h * 128:(h + 1) * 128, :],
                             waits=[("pe", slot_pe[sl])] + qsp_w,
                             inc="haA%d" % sl)
            if is_a:
                g2 = 2 * (h // 4)
                ro = (h % 4) * 32
                s.dma("scalar", s.RQB[0:32, sl * 1024:(sl + 1) * 1024],
                      s.qsp[D + g2 * 128 + ro: D + g2 * 128 + ro + 32, :],
                      waits=[("pe", slot_pe[sl])] + qsp_w, inc="haA%d" % sl)
                q_ld[sl] = s.dma("scalar",
                                 s.RQB[32:64, sl * 1024:(sl + 1) * 1024],
                                 s.qsp[D + (g2 + 1) * 128 + ro:
                                       D + (g2 + 1) * 128 + ro + 32, :],
                                 waits=[("pe", slot_pe[sl])] + qsp_w,
                                 inc="haA%d" % sl)

        def flush():
            if pending[0] is None:
                return
            (kcl, pbsl, qoff, se_e, ve_m, sl, po_sl, firsts, lasts, h, g) = pending[0]
            pending[0] = None
            pe_c = None
            for idx, (uu, rr) in enumerate(kcl):
                pslice = s.PBuf[:, pbsl * 1024 + idx * 512 + qoff:
                                pbsl * 1024 + (idx + 1) * 512]
                w = [("se", se_e)] + ([("ve", ve_m)] if ve_m else [])
                if firsts[idx]:
                    w += [("ve", self.pl_free), ("ve", po_free[po_sl]),
                          ("ngm", ngm)]
                s.op("tensor", lambda q, pslice=pslice, qoff=qoff,
                     fs=firsts[idx], lt=lasts[idx]:
                     q.matmul(s.PLp[0:1, qoff:512], s.ONECBF[:, :], pslice,
                              start=fs, stop=lt), waits=w, inc=None)
                vcol = (rr * 8 + uu) * 128
                pe_c = s.op("tensor", lambda q, pslice=pslice, vcol=vcol, sl=sl,
                            po_sl=po_sl, qoff=qoff, fs=firsts[idx], lt=lasts[idx]:
                            q.matmul(s.PO[po_sl][:, qoff:512],
                                     vbuf[sl][:, vcol:vcol + 128], pslice,
                                     start=fs, stop=lt),
                            waits=[("haA%d" % sl, v_ld[sl])], inc="pe")
            pb_free[pbsl] = pe_c
            if lasts[-1]:
                ve_r = s.op("vector", lambda q: q.reciprocal(
                    s.RLr[0:1, :], s.PLp[0:1, :]), waits=[("pe", pe_c)], inc="ve")
                self.pl_free = ve_r
                pe_b = s.op("tensor", lambda q: q.matmul(
                    s.PXp[:, 0:512], s.ONER32[:, :], s.RLr[0:1, :],
                    start=True, stop=True),
                    waits=[("ve", ve_r), ("ve", self.px_free)], inc="pe")
                ve_c = s.op("vector", lambda q: q.tensor_copy(
                    s.RLB[:, :], s.PXp[:, 0:512]), waits=[("pe", pe_b)], inc="ve")
                self.px_free = ve_c
                ve_at = s.op("vector", lambda q, h=h, g=g, po_sl=po_sl:
                             q.tensor_tensor(
                                 s.AT[:, h * 1024 + g * 512: h * 1024 + (g + 1) * 512],
                                 s.PO[po_sl], s.RLB[:, :], ALU.mult),
                             waits=[("ve", ve_c), ("pe", attnt_pe)], inc="ve")
                po_free[po_sl] = ve_at

        load_head(0)
        load_head(1)
        for h in range(H):
            sl = h % 2
            for g in range(2):
                po_sl = (h * 2 + g) % 2
                kcs = [(u, r) for u in range(4 * g + 4) for r in range(4)]
                pairs = [(kcs[i], kcs[i + 1]) for i in range(0, len(kcs), 2)]
                npair = len(pairs)
                for pi, (kc0, kc1) in enumerate(pairs):
                    par = gpair[0] % 2
                    pbsl = gpair[0] % 2
                    gpair[0] += 1
                    u, _ = kc0
                    t0 = max(4 * g, u)
                    qoff = (t0 - 4 * g) * 128
                    pe_sc = None
                    for half, (uu, rr) in enumerate((kc0, kc1)):
                        kcol = rr * 1024 + uu * 128
                        dst = s.PPAIR[par][:, half * 512 + qoff:(half + 1) * 512]
                        qs = s.AT[:, h * 1024 + g * 512 + qoff:
                                  h * 1024 + (g + 1) * 512]
                        w0 = [("hsA%d" % sl, k_ld[sl]), ("haA%d" % sl, q_ld[sl]),
                              ("se", ps_free[par])]
                        if is_a:
                            s.op("tensor", lambda q, dst=dst, kcol=kcol, qs=qs,
                                 sl=sl: q.matmul(dst, kbuf[sl][:, kcol:kcol + 128],
                                                 qs, start=True, stop=False),
                                 waits=w0, inc=None)
                            rqs = s.RQB[0:RD, sl * 1024 + g * 512 + qoff:
                                        sl * 1024 + (g + 1) * 512]
                            pe_sc = s.op("tensor", lambda q, dst=dst, rqs=rqs,
                                         rr=rr, uu=uu:
                                         q.matmul(dst,
                                                  s.RKG[0:RD, rr * 1024 + uu * 128:
                                                        rr * 1024 + (uu + 1) * 128],
                                                  rqs, start=False, stop=True),
                                         waits=[("hsG", rkg_hs)],
                                         inc="pe" if half == 1 else None)
                        else:
                            pe_sc = s.op("tensor", lambda q, dst=dst, kcol=kcol,
                                         qs=qs, sl=sl:
                                         q.matmul(dst, kbuf[sl][:, kcol:kcol + 128],
                                                  qs, start=True, stop=True),
                                         waits=w0, inc="pe" if half == 1 else None)
                    flush()
                    negcol = s.NEGM[:, g * 16 + h: g * 16 + h + 1]
                    srcp = s.PPAIR[par].rearrange("p (hf c) -> p hf c", hf=2)[
                        :, :, qoff:512]
                    dstp = s.PBuf[:, pbsl * 1024:(pbsl + 1) * 1024].rearrange(
                        "p (hf c) -> p hf c", hf=2)[:, :, qoff:512]
                    se_e = s.op("scalar", lambda q, srcp=srcp, dstp=dstp,
                                negcol=negcol:
                                q.activation(dstp, srcp, AF.Exp, bias=negcol),
                                waits=[("pe", pe_sc), ("ngm", ngm),
                                       ("pe", pb_free[pbsl])], inc="se")
                    ps_free[par] = se_e
                    ve_m = None
                    if u >= 4 * g:
                        for half, (uu, rr) in enumerate((kc0, kc1)):
                            psl = s.PBuf[:, pbsl * 1024 + half * 512 + qoff:
                                         pbsl * 1024 + half * 512 + qoff + 128]
                            ve_m = s.op("vector", lambda q, psl=psl, rr=rr:
                                        q.tensor_tensor(
                                            psl, psl,
                                            s.MSK[:, rr * 128:(rr + 1) * 128],
                                            ALU.mult),
                                        waits=[("se", se_e), ("gwI", self.msk_gw)],
                                        inc="ve")
                    firsts = [pi == 0, False]
                    lasts = [False, pi == npair - 1]
                    pending[0] = ((kc0, kc1), pbsl, qoff, se_e, ve_m, sl, po_sl,
                                  firsts, lasts, h, g)
            if h + 2 < H:
                flush()
                slot_pe[sl] = s.n("pe")
                load_head(h + 2)
        flush()
        return s.n("ve")

    def emit_attn_a(self, r_ve, sa_free_pe, tile_w=None):
        s = self
        xln_ve = s.emit_ln(r_ve, sa_free_pe, tile_w)

        bc = s.bcol("b_dkv", s.wa["b_dkv"], 4)
        st_dkv = s.wstream_straight(s.wsa["w_dkv"], DC, 512, 1, [s.ca["w_dkv"]])

        def epi_lkv(m, nn2, slp, pe, bc=bc):
            return s.op("scalar", lambda q, m=m, nn2=nn2, slp=slp: q.activation(
                s.AT[:, 12288 + m * 1024 + nn2 * 512: 12288 + m * 1024 + (nn2 + 1) * 512],
                s.PSLOT[slp], AF.Identity, bias=s.BCOL[:, m:m + 1]),
                waits=[("pe", pe), ("ve", bc)], inc="se")

        s.proj_feat(DC, st_dkv, 512, 1, lambda k, nn2: self.xlnt(k, nn2 * 512, 512),
                    epi_lkv, extra_w=[("ve", xln_ve)])
        lkv_pe, lkv_se = s.n("pe"), s.n("se")

        def lkvt(k, c0, nn):
            return s.AT[:, 12288 + k * 1024 + c0: 12288 + k * 1024 + c0 + nn]

        def lqt(k, c0, nn):
            return s.AT[:, k * 1024 + c0: k * 1024 + c0 + nn]

        st_rk = s.wstream_straight(s.wsa["w_rk"], DC, RD, 1, [s.ca["w_rk"]])
        st_rk.prefetch(0)
        reg_rk, ld_rk = st_rk.get(0)
        rkrow, rksem, rkrow_gw = s.load_row("b_rk", s.wa["b_rk"], RD)
        pe = None
        for half in range(2):
            pe = s.op("tensor", lambda q, half=half: q.matmul(
                s.PXp[0:32, 30 + half:31 + half], rkrow(half * 32, 32),
                s.ONE11B[:, :], start=True, stop=True),
                waits=[("hs", self.init_hs), (rksem, rkrow_gw),
                       ("ve", self.px_free)], inc="pe")
        bk_ve = s.op("vector", lambda q: q.tensor_copy(
            s.BCOL[0:32, 14:16], s.PXp[0:32, 30:32]), waits=[("pe", pe)], inc="ve")
        self.px_free = bk_ve
        s.row_done("b_rk", "pe", s.n("pe"))
        s.op("vector", lambda q: q.memset(s.NRM[0:1, 16:17], 0.0), inc="ve")
        rk_free = [("se", lkv_se), ("se", lkv_se)]
        sq_pe = s.n("pe")
        for nn2 in range(2):
            ctw = [("ve", self.ctb_free)] if self.ctb_free else []
            s.dma("sync", s.CTB[:, 0:512],
                  s.cos_in[:, nn2 * 512:(nn2 + 1) * 512], waits=ctw, inc="hsT")
            s.dma("sync", s.CTB[:, 512:1024],
                  s.sin_in[:, nn2 * 512:(nn2 + 1) * 512], waits=ctw, inc="hsT")
            tw = s.n("hsT")
            pes = [None, None]
            for half in range(2):
                for k in range(DC):
                    w = []
                    if k == 0:
                        w = [rk_free[half], ("ve", xln_ve)]
                    if k % 4 == 0:
                        w.append(MK.qgate(ld_rk, k))
                    pes[half] = s.op("tensor", lambda q, k=k, half=half, nn2=nn2,
                                     fst=(k == 0), lst=(k == DC - 1), reg=reg_rk:
                                     q.matmul(s.PSLOT[half][0:32, :],
                                              s.WB[:, reg + k * RD + half * 32:
                                                   reg + k * RD + (half + 1) * 32],
                                              self.xlnt(k, nn2 * 512, 512),
                                              start=fst, stop=lst),
                                     waits=w, inc="pe" if k == DC - 1 else None)
            fw = [self.stg_free[0]] if self.stg_free[0][0] else []
            v1 = s.op("vector", lambda q: q.scalar_tensor_tensor(
                s.RLB[0:32, :], s.PSLOT[0][0:32, :], s.BCOL[0:32, 14:15],
                s.CTB[0:32, 0:512], ALU.add, ALU.mult),
                waits=[("pe", pes[0]), ("pe", pes[1]), ("ve", bk_ve),
                       ("hsT", tw)], inc="ve")
            v2 = s.op("vector", lambda q: q.scalar_tensor_tensor(
                s.TMPB[0:32, :], s.PSLOT[1][0:32, :], s.BCOL[0:32, 15:16],
                s.CTB[0:32, 512:1024], ALU.add, ALU.mult),
                waits=[("pe", sq_pe)], inc="ve")
            v3 = s.op("vector", lambda q: q.tensor_tensor(
                s.STG[0:32, 0:512], s.RLB[0:32, :], s.TMPB[0:32, :],
                ALU.subtract), waits=fw, inc="ve")
            v4 = s.op("vector", lambda q: q.scalar_tensor_tensor(
                s.RLB[0:32, :], s.PSLOT[0][0:32, :], s.BCOL[0:32, 14:15],
                s.CTB[0:32, 512:1024], ALU.add, ALU.mult), inc="ve")
            v5 = s.op("vector", lambda q: q.scalar_tensor_tensor(
                s.TMPB[0:32, :], s.PSLOT[1][0:32, :], s.BCOL[0:32, 15:16],
                s.CTB[0:32, 0:512], ALU.add, ALU.mult), inc="ve")
            v6 = s.op("vector", lambda q: q.tensor_tensor(
                s.STG[0:32, 512:1024], s.RLB[0:32, :], s.TMPB[0:32, :],
                ALU.add), inc="ve")
            rk_free = [("ve", v4), ("ve", v5)]
            self.ctb_free = v5
            hs = s.dma("sync", s.ckr_a[0:32, nn2 * 512:(nn2 + 1) * 512],
                       s.STG[0:32, 0:512], waits=[("ve", v3)], inc="hsC0")
            hs = s.dma("sync", s.ckr_a[32:RD, nn2 * 512:(nn2 + 1) * 512],
                       s.STG[0:32, 512:1024], waits=[("ve", v6)], inc="hsC0")
            self.stg_free[0] = ("hsC0", hs)
            vsq = s.op("vector", lambda q: q.scalar_tensor_tensor(
                s.SQB[0:32, 0:1024], s.STG[0:32, 0:1024], 1.0,
                s.STG[0:32, 0:1024], ALU.mult, ALU.mult),
                waits=[("ve", v6)], inc="ve")
            pe2 = None
            for g in range(2):
                pe2 = s.op("tensor", lambda q, g=g: q.matmul(
                    s.PLp[0:1, :], s.ONECBF[0:32, :],
                    s.SQB[0:32, g * 512:(g + 1) * 512],
                    start=(g == 0), stop=(g == 1)),
                    waits=[("ve", vsq), ("ve", self.pl_free)],
                    inc="pe" if g == 1 else None)
            sq_pe = pe2
            rv = s.op("vector", lambda q: q.tensor_reduce(
                s.NRM[0:1, 343:344], s.PLp[0:1, :], AX.X, ALU.max),
                waits=[("pe", pe2)], inc="ve")
            self.pl_free = s.op("vector", lambda q: q.tensor_tensor(
                s.NRM[0:1, 16:17], s.NRM[0:1, 16:17], s.NRM[0:1, 343:344],
                ALU.max), waits=[("ve", rv)], inc="ve")
        rk_sq_pe = sq_pe
        cw_r = [("hsC0", s.n("hsC0"))]

        bc = s.bcol("b_uk", s.wa["b_uk"], DC)
        st_uk = s.wstream_straight(s.wsa["w_uk"], 4, 512, 4, [s.ca["w_uk"]])
        cwk = {}

        def epi_k(m, nn2, slp, pe, bc=bc):
            stg = m % 2
            fw = [self.stg_free[stg]] if self.stg_free[stg][0] else []
            se = s.op("scalar", lambda q, nn2=nn2, slp=slp, stg=stg, m=m:
                      q.activation(
                          s.STG[:, stg * 1024 + nn2 * 512: stg * 1024 + (nn2 + 1) * 512],
                          s.PSLOT[slp], AF.Identity, bias=s.BCOL[:, m:m + 1]),
                      waits=[("pe", pe), ("ve", bc)] + fw, inc="se")
            if nn2 == 1:
                vq = s.op("vector", lambda q, stg=stg: q.scalar_tensor_tensor(
                    s.SQB[:, 0:1024], s.STG[:, stg * 1024:(stg + 1) * 1024], 1.0,
                    s.STG[:, stg * 1024:(stg + 1) * 1024], ALU.mult, ALU.mult),
                    waits=[("se", se)], inc="ve")
                self.head_norm(lambda g: s.SQB[:, g * 512:(g + 1) * 512], 17 + m,
                               extra_w=[("ve", vq)])
                hs = s.dma("sync", s.ck_a[m // 4][(m % 4) * 128:
                                                  (m % 4 + 1) * 128, :],
                           s.STG[:, stg * 1024:(stg + 1) * 1024],
                           waits=[("se", se)], inc="hsC%d" % stg)
                self.stg_free[stg] = ("hsC%d" % stg, hs)
                if m % 4 == 3:
                    cwk[m // 4] = [("hsC0", s.n("hsC0")),
                                   ("hsC1", s.n("hsC1"))]
            return se

        s.proj_feat(4, st_uk, 512, 4, lambda k, nn2: lkvt(k, nn2 * 512, 512),
                    epi_k, extra_w=[("pe", lkv_pe), ("se", lkv_se)],
                    init_free=[rk_free[0], rk_free[1],
                               ("se", lkv_se), ("se", lkv_se)])
        nv = s.op("vector", lambda q: q.tensor_copy(
            s.KROW[0:1, 0:16], s.NRM[0:1, 17:33]),
            waits=[("ve", s.n("ve"))], inc="ve")
        nv = s.op("vector", lambda q: q.tensor_copy(
            s.KROW[0:1, 16:17], s.NRM[0:1, 16:17]), waits=[("ve", nv)], inc="ve")
        s.dma("sync", s.ckn_a[0:1, :], s.KROW[0:1, :],
              waits=[("ve", nv)], inc="hsK")
        cw_n = [("hsK", s.n("hsK"))]
        ccr = s.op("gpsimd", lambda q: q.collective_compute(
            "AllGather", ALU.bypass, replica_groups=GROUPS,
            ins=[s.ckr_a], outs=[s.gkr_a]), waits=cw_r, inc="cg")
        cck = {}
        for g in range(4):
            cck[g] = s.op("gpsimd", lambda q, g=g: q.collective_compute(
                "AllGather", ALU.bypass, replica_groups=GROUPS,
                ins=[s.ck_a[g]], outs=[s.gk_a[g]]), waits=cwk[g], inc="cg")
        ccn = s.op("gpsimd", lambda q: q.collective_compute(
            "AllGather", ALU.bypass, replica_groups=GROUPS,
            ins=[s.ckn_a], outs=[s.gkn_a]), waits=cw_n, inc="cg")
        s.dma("gpsimd", s.wsa["w_o"], s.wa["w_o"],
              waits=[("gwC", s.n("gwC"))], inc="gwC")
        s.ca["w_o"] = ("gwC", s.n("gwC"))

        st_uv = s.wstream_straight(s.wsa["w_uv"], 4, 512, 4, [s.ca["w_uv"]])
        uvrow, uvsem, uvrow_gw = s.load_row("b_uv", s.wa["b_uv"], D)
        cwv = {}

        def epi_v(si, t, slp, pe):
            stg = t % 2
            fw = [self.stg_free[stg]] if self.stg_free[stg][0] else []
            se = s.op("scalar", lambda q, slp=slp, stg=stg: q.activation(
                s.STG[:, stg * 1024: stg * 1024 + 512], s.PSLOT[slp], AF.Identity),
                waits=[("pe", pe)] + fw, inc="se")
            hs = s.dma("sync",
                       s.cv_a[si][:, t * 128:(t + 1) * 128]
                       .rearrange("(h p) d -> p h d", p=128),
                       s.STG[:, stg * 1024: stg * 1024 + 512]
                       .rearrange("p (h d) -> p h d", d=128),
                       waits=[("se", se)], inc="hsC%d" % stg)
            self.stg_free[stg] = ("hsC%d" % stg, hs)
            if t == NT - 1:
                cwv[si] = [("hsC0", s.n("hsC0")), ("hsC1", s.n("hsC1"))]
            return ("se", se)

        s.proj_tok(4, st_uv, 512, 4, lambda k, t: lkvt(k, t * 128, 128),
                   uvrow, [(uvsem, uvrow_gw)], epi_v, extra_w=[("se", lkv_se)])
        s.row_done("b_uv", "pe", s.n("pe"))
        ccv = {}
        for g in range(4):
            ccv[g] = s.op("gpsimd", lambda q, g=g: q.collective_compute(
                "AllGather", ALU.bypass, replica_groups=GROUPS,
                ins=[s.cv_a[g]], outs=[s.gv_a[g]]), waits=cwv[g], inc="cg")
        kw = [("cg", cck[g]) for g in range(4)]
        vw = [("cg", ccv[g]) for g in range(4)]
        self.emit_casts()

        bc = s.bcol("b_dq", s.wa["b_dq"], 12)
        st_dq = s.wstream_straight(s.wsa["w_dq"], DC, 512, 3, [s.ca["w_dq"]])

        def epi_lq(m, nn2, slp, pe, bc=bc):
            return s.op("scalar", lambda q, m=m, nn2=nn2, slp=slp: q.activation(
                s.AT[:, m * 1024 + nn2 * 512: m * 1024 + (nn2 + 1) * 512],
                s.PSLOT[slp], AF.Identity, bias=s.BCOL[:, m:m + 1]),
                waits=[("pe", pe), ("ve", bc)], inc="se")

        s.proj_feat(DC, st_dq, 512, 3, lambda k, nn2: self.xlnt(k, nn2 * 512, 512),
                    epi_lq)
        lq_pe = s.n("pe")

        lq_se0 = s.n("se")
        st_rq = s.wstream_straight(s.wsa["w_rq"], 12, 512, 2, [s.ca["w_rq"]])
        st_rq.prefetch(0)
        st_rq.prefetch(1)
        bcq = s.bcol("b_rq", s.wa["b_rq"], 8, scale=SA_A)
        scl = {}
        for si in range(2):
            reg, ld = st_rq.get(si)
            scl[si] = s.op("vector", lambda q, reg=reg: q.tensor_scalar(
                s.WB[:, reg:reg + 12 * 512], s.WB[:, reg:reg + 12 * 512],
                SA_A, None, ALU.mult), waits=MK.qall(ld), inc="ve")
        rqm_ve = s.op("vector", lambda q: q.memset(s.NRM[0:1, 34:35], 0.0),
                      inc="ve")
        sfree = [("se", lq_se0)] * 4
        sq_pe = s.n("pe")
        i = 0
        for nn2 in range(2):
            ctw = [("ve", self.ctb_free)] if self.ctb_free else []
            s.dma("sync", s.CTB[:, 0:512],
                  s.cos_in[:, nn2 * 512:(nn2 + 1) * 512], waits=ctw, inc="hsT")
            s.dma("sync", s.CTB[:, 512:1024],
                  s.sin_in[:, nn2 * 512:(nn2 + 1) * 512], waits=ctw, inc="hsT")
            tw = s.n("hsT")
            for si in range(2):
                reg, _ = st_rq.get(si)
                for pr in range(2):
                    c = si * 2 + pr
                    sE, sO = (2 * i) % 4, (2 * i + 1) % 4
                    i += 1
                    pes = {}
                    for half, slp in ((0, sE), (1, sO)):
                        ml = 2 * pr + half
                        pe = None
                        for k in range(12):
                            w = []
                            if k == 0:
                                w = [sfree[slp], ("ve", scl[si])]
                            pe = s.op("tensor", lambda q, k=k, ml=ml, slp=slp,
                                      reg=reg, nn2=nn2, fst=(k == 0),
                                      lst=(k == 11):
                                      q.matmul(s.PSLOT[slp],
                                               s.WB[:, reg + k * 512 + ml * 128:
                                                    reg + k * 512 + (ml + 1) * 128],
                                               lqt(k, nn2 * 512, 512),
                                               start=fst, stop=lst),
                                      waits=w, inc="pe" if k == 11 else None)
                        pes[half] = pe
                    stg = i % 2
                    st0 = stg * 1024
                    fw = [self.stg_free[stg]] if self.stg_free[stg][0] else []
                    bE = 2 * c
                    v1 = s.op("vector", lambda q, sE=sE, bE=bE:
                              q.scalar_tensor_tensor(
                                  s.RLB[:, :], s.PSLOT[sE], s.BCOL[:, bE:bE + 1],
                                  s.CTB[:, 0:512], ALU.add, ALU.mult),
                              waits=[("pe", pes[0]), ("pe", pes[1]), ("ve", bcq),
                                     ("hsT", tw)], inc="ve")
                    v2 = s.op("vector", lambda q, sO=sO, bE=bE:
                              q.scalar_tensor_tensor(
                                  s.TMPB[:, :], s.PSLOT[sO],
                                  s.BCOL[:, bE + 1:bE + 2],
                                  s.CTB[:, 512:1024], ALU.add, ALU.mult),
                              waits=[("pe", sq_pe)], inc="ve")
                    v3 = s.op("vector", lambda q, st0=st0: q.tensor_tensor(
                        s.STG[:, st0:st0 + 512], s.RLB[:, :], s.TMPB[:, :],
                        ALU.subtract), waits=fw, inc="ve")
                    v4 = s.op("vector", lambda q, sE=sE, bE=bE:
                              q.scalar_tensor_tensor(
                                  s.RLB[:, :], s.PSLOT[sE], s.BCOL[:, bE:bE + 1],
                                  s.CTB[:, 512:1024], ALU.add, ALU.mult), inc="ve")
                    v5 = s.op("vector", lambda q, sO=sO, bE=bE:
                              q.scalar_tensor_tensor(
                                  s.TMPB[:, :], s.PSLOT[sO],
                                  s.BCOL[:, bE + 1:bE + 2],
                                  s.CTB[:, 0:512], ALU.add, ALU.mult), inc="ve")
                    v6 = s.op("vector", lambda q, st0=st0: q.tensor_tensor(
                        s.STG[:, st0 + 512:st0 + 1024], s.RLB[:, :], s.TMPB[:, :],
                        ALU.add), inc="ve")
                    sfree[sE] = ("ve", v4)
                    sfree[sO] = ("ve", v5)
                    self.ctb_free = v5
                    hs = s.dma("sync",
                               s.qsp[D + 2 * c * 128:D + (2 * c + 1) * 128,
                                     nn2 * 512:(nn2 + 1) * 512],
                               s.STG[:, st0:st0 + 512], waits=[("ve", v3)],
                               inc="hsC%d" % stg)
                    hs = s.dma("sync",
                               s.qsp[D + (2 * c + 1) * 128:D + (2 * c + 2) * 128,
                                     nn2 * 512:(nn2 + 1) * 512],
                               s.STG[:, st0 + 512:st0 + 1024], waits=[("ve", v6)],
                               inc="hsC%d" % stg)
                    self.stg_free[stg] = ("hsC%d" % stg, hs)
                    vsq = s.op("vector", lambda q, st0=st0:
                               q.scalar_tensor_tensor(
                                   s.SQB[:, 0:1024], s.STG[:, st0:st0 + 1024], 1.0,
                                   s.STG[:, st0:st0 + 1024], ALU.mult, ALU.mult),
                               waits=[("ve", v6)], inc="ve")
                    pe2 = None
                    for g in range(2):
                        pe2 = s.op("tensor", lambda q, g=g: q.matmul(
                            s.PLp[0:1, :], s.ONECBF[:, :],
                            s.SQB[:, g * 512:(g + 1) * 512],
                            start=(g == 0), stop=(g == 1)),
                            waits=[("ve", vsq), ("ve", self.pl_free)],
                            inc="pe" if g == 1 else None)
                    sq_pe = pe2
                    rv = s.op("vector", lambda q: q.tensor_reduce(
                        s.NRM[0:1, 343:344], s.PLp[0:1, :], AX.X, ALU.max),
                        waits=[("pe", pe2)], inc="ve")
                    rqm_ve = s.op("vector", lambda q: q.tensor_tensor(
                        s.NRM[0:1, 34:35], s.NRM[0:1, 34:35],
                        s.NRM[0:1, 343:344], ALU.max),
                        waits=[("ve", rv), ("ve", rqm_ve)], inc="ve")
                    self.pl_free = rqm_ve
        bc = s.bcol("b_uq", s.wa["b_uq"], DC, scale=SA_A)
        st_uq = s.wstream_straight(s.wsa["w_uq"], 12, 512, 4, [s.ca["w_uq"]])

        def epi_q(m, nn2, slp, pe, bc=bc):
            stg = m % 2
            fw = [self.stg_free[stg]] if self.stg_free[stg][0] else []
            se = s.op("scalar", lambda q, nn2=nn2, slp=slp, stg=stg, m=m:
                      q.activation(
                          s.STG[:, stg * 1024 + nn2 * 512: stg * 1024 + (nn2 + 1) * 512],
                          s.PSLOT[slp], AF.Identity, bias=s.BCOL[:, m:m + 1],
                          scale=SA_A),
                      waits=[("pe", pe), ("ve", bc)] + fw, inc="se")
            if nn2 == 1:
                vq = s.op("vector", lambda q, stg=stg: q.scalar_tensor_tensor(
                    s.SQB[:, 0:1024], s.STG[:, stg * 1024:(stg + 1) * 1024], 1.0,
                    s.STG[:, stg * 1024:(stg + 1) * 1024], ALU.mult, ALU.mult),
                    waits=[("se", se)], inc="ve")
                for g in range(2):
                    pe2 = s.op("tensor", lambda q, g=g: q.matmul(
                        s.PLp[0:1, :], s.ONECBF[:, :],
                        s.SQB[:, g * 512:(g + 1) * 512], start=True, stop=True),
                        waits=[("ve", vq), ("ve", self.pl_free)], inc="pe")
                    self.pl_free = s.op("vector", lambda q, g=g, m=m:
                                        q.tensor_reduce(
                                            s.NRM[0:1, 64 + g * 16 + m:
                                                  65 + g * 16 + m],
                                            s.PLp[0:1, :], AX.X, ALU.max),
                                        waits=[("pe", pe2)], inc="ve")
                hs = s.dma("sync", s.qsp[m * 128:(m + 1) * 128, :],
                           s.STG[:, stg * 1024:(stg + 1) * 1024],
                           waits=[("se", se)], inc="hsC%d" % stg)
                self.stg_free[stg] = ("hsC%d" % stg, hs)
            return se

        s.proj_feat(12, st_uq, 512, 4, lambda k, nn2: lqt(k, nn2 * 512, 512),
                    epi_q, init_free=sfree)
        q_done_pe = s.n("pe")
        qsp_w = [("hsC0", s.n("hsC0")), ("hsC1", s.n("hsC1"))]
        qn_ve = s.n("ve")

        ngm = self.emit_negm(s.gkn_a, True, [("cg", ccn)], qn_ve)
        st_o = s.wstream_straight(s.wsa["w_o"], DC, 512, 4, [s.ca["w_o"]])
        orow, osem, orow_gw = s.load_row("b_o", s.wa["b_o"], D)
        att_ve = self.emit_attention(True, s.gk_a, s.gv_a, kw, vw,
                                     [("cg", ccr)], ngm, q_done_pe, qsp_w)
        rve = self.emit_out_proj(
            lambda k, t: s.AT[:, k * 1024 + t * 128: k * 1024 + (t + 1) * 128],
            st_o, orow, [(osem, orow_gw)], extra_w=[("ve", att_ve)],
            ln_next=True)
        s.row_done("b_o", "pe", s.n("pe"))
        return rve, s.n("pe")

    def emit_ffn(self, r_ve, sa_free_pe, ln_next=False):
        s = self
        if self.pending_xln is not None:
            xln_ve, self.pending_xln = self.pending_xln, None
        else:
            xln_ve = s.emit_ln(r_ve, sa_free_pe)
        srow, ssem, sgw = s.load_row("f_s", s.wf["s"], D)

        def post_scale(reg, si, ld):
            pe = s.op("tensor", lambda q, si=si: q.matmul(
                s.PXp[:, 0:512], s.ONERBF[:, :], srow(si * 512, 512),
                start=True, stop=True),
                waits=[("hs", self.init_hs), (ssem, sgw),
                       ("ve", self.px_free), ("ve", self.sslb_free)], inc="pe")
            cp = s.op("vector", lambda q: q.tensor_copy(
                s.SSLB[:, :], s.PXp[:, 0:512]), waits=[("pe", pe)], inc="ve")
            self.px_free = cp
            ve = None
            for k in range(DC):
                ve = s.op("vector", lambda q, reg=reg, k=k: q.tensor_tensor(
                    s.WB[:, reg + k * 512: reg + (k + 1) * 512],
                    s.WB[:, reg + k * 512: reg + (k + 1) * 512],
                    s.SSLB[:, :], ALU.mult),
                    waits=[(ld[0], ld[1]), ("ve", cp)], inc="ve")
            self.sslb_free = ve
            return ("ve", ve)

        self.sslb_free = s.n("ve")
        st = s.wstream_T(s.wsc["f"], 512, 4, self.pc["f"], post=post_scale)
        frow, fsem, fgw = s.load_row("f_b", s.wf["b"], D)
        rve = self.emit_out_proj(
            lambda k, t: self.xlnt(k, t * 128, 128), st, frow,
            [(fsem, fgw)], extra_w=[("ve", xln_ve)], ln_next=ln_next)
        s.row_done("f_b", "pe", s.n("pe"))
        s.row_done("f_s", "pe", s.n("pe"))
        return rve, s.n("pe")

    def emit_attn_b(self, r_ve, sa_free_pe):
        s = self
        if self.pending_xln is not None:
            xln_ve, self.pending_xln = self.pending_xln, None
        else:
            xln_ve = s.emit_ln(r_ve, sa_free_pe)

        bc = s.bcol("bk", s.wbq["bk"], DC, scale=SA_B)
        st_k = s.wstream_T(s.wsc["k"], 512, 4, self.pc["k"])
        cwk = {}

        def epi_k(m, nn2, slp, pe, bc=bc):
            stg = m % 2
            fw = [self.stg_free[stg]] if self.stg_free[stg][0] else []
            se = s.op("scalar", lambda q, nn2=nn2, slp=slp, stg=stg, m=m:
                      q.activation(
                          s.STG[:, stg * 1024 + nn2 * 512: stg * 1024 + (nn2 + 1) * 512],
                          s.PSLOT[slp], AF.Identity, bias=s.BCOL[:, m:m + 1],
                          scale=SA_B),
                      waits=[("pe", pe), ("ve", bc)] + fw, inc="se")
            if nn2 == 1:
                vq = s.op("vector", lambda q, stg=stg: q.scalar_tensor_tensor(
                    s.SQB[:, 0:1024], s.STG[:, stg * 1024:(stg + 1) * 1024], 1.0,
                    s.STG[:, stg * 1024:(stg + 1) * 1024], ALU.mult, ALU.mult),
                    waits=[("se", se)], inc="ve")
                self.head_norm(lambda g: s.SQB[:, g * 512:(g + 1) * 512], 17 + m,
                               extra_w=[("ve", vq)])
                hs = s.dma("sync", s.ck_b[m // 4][(m % 4) * 128:
                                                  (m % 4 + 1) * 128, :],
                           s.STG[:, stg * 1024:(stg + 1) * 1024],
                           waits=[("se", se)], inc="hsC%d" % stg)
                self.stg_free[stg] = ("hsC%d" % stg, hs)
                if m % 4 == 3:
                    cwk[m // 4] = [("hsC0", s.n("hsC0")),
                                   ("hsC1", s.n("hsC1"))]
            return se

        s.proj_feat(DC, st_k, 512, 4, lambda k, nn2: self.xlnt(k, nn2 * 512, 512),
                    epi_k, extra_w=[("ve", xln_ve)])
        nv = s.op("vector", lambda q: q.tensor_copy(
            s.KROW[0:1, 0:16], s.NRM[0:1, 17:33]),
            waits=[("ve", s.n("ve"))], inc="ve")
        s.dma("sync", s.ckn_b[0:1, :], s.KROW[0:1, :],
              waits=[("ve", nv)], inc="hsK")
        cw_n = [("hsK", s.n("hsK"))]
        cck = {}
        for g in range(4):
            cck[g] = s.op("gpsimd", lambda q, g=g: q.collective_compute(
                "AllGather", ALU.bypass, replica_groups=GROUPS,
                ins=[s.ck_b[g]], outs=[s.gk_b[g]]), waits=cwk[g], inc="cg")
        ccn = s.op("gpsimd", lambda q: q.collective_compute(
            "AllGather", ALU.bypass, replica_groups=GROUPS,
            ins=[s.ckn_b], outs=[s.gkn_b]), waits=cw_n, inc="cg")

        st_v = s.wstream_T(s.wsc["v"], 512, 4, self.pc["v"])
        bvrow, bvsem, bvrow_gw = s.load_row("bv", s.wbq["bv"], D)
        cwv = {}

        def epi_v(si, t, slp, pe):
            stg = t % 2
            fw = [self.stg_free[stg]] if self.stg_free[stg][0] else []
            se = s.op("scalar", lambda q, slp=slp, stg=stg: q.activation(
                s.STG[:, stg * 1024: stg * 1024 + 512], s.PSLOT[slp], AF.Identity),
                waits=[("pe", pe)] + fw, inc="se")
            hs = s.dma("sync",
                       s.cv_b[si][:, t * 128:(t + 1) * 128]
                       .rearrange("(h p) d -> p h d", p=128),
                       s.STG[:, stg * 1024: stg * 1024 + 512]
                       .rearrange("p (h d) -> p h d", d=128),
                       waits=[("se", se)], inc="hsC%d" % stg)
            self.stg_free[stg] = ("hsC%d" % stg, hs)
            if t == NT - 1:
                cwv[si] = [("hsC0", s.n("hsC0")), ("hsC1", s.n("hsC1"))]
            return ("se", se)

        s.proj_tok(DC, st_v, 512, 4, lambda k, t: self.xlnt(k, t * 128, 128),
                   bvrow, [(bvsem, bvrow_gw)], epi_v)
        s.row_done("bv", "pe", s.n("pe"))
        ccv = {}
        for g in range(4):
            ccv[g] = s.op("gpsimd", lambda q, g=g: q.collective_compute(
                "AllGather", ALU.bypass, replica_groups=GROUPS,
                ins=[s.cv_b[g]], outs=[s.gv_b[g]]), waits=cwv[g], inc="cg")
        kw = [("cg", cck[g]) for g in range(4)]
        vw = [("cg", ccv[g]) for g in range(4)]

        bc = s.bcol("bq", s.wbq["bq"], DC, scale=SA_B)
        st_q = s.wstream_T(s.wsc["q"], 512, 4, self.pc["q"])

        def epi_q(m, nn2, slp, pe, bc=bc):
            stg = m % 2
            fw = [self.stg_free[stg]] if self.stg_free[stg][0] else []
            se = s.op("scalar", lambda q, nn2=nn2, slp=slp, stg=stg, m=m:
                      q.activation(
                          s.STG[:, stg * 1024 + nn2 * 512: stg * 1024 + (nn2 + 1) * 512],
                          s.PSLOT[slp], AF.Identity, bias=s.BCOL[:, m:m + 1],
                          scale=SA_B),
                      waits=[("pe", pe), ("ve", bc)] + fw, inc="se")
            if nn2 == 1:
                vq = s.op("vector", lambda q, stg=stg: q.scalar_tensor_tensor(
                    s.SQB[:, 0:1024], s.STG[:, stg * 1024:(stg + 1) * 1024], 1.0,
                    s.STG[:, stg * 1024:(stg + 1) * 1024], ALU.mult, ALU.mult),
                    waits=[("se", se)], inc="ve")
                for g in range(2):
                    pe2 = s.op("tensor", lambda q, g=g: q.matmul(
                        s.PLp[0:1, :], s.ONECBF[:, :],
                        s.SQB[:, g * 512:(g + 1) * 512], start=True, stop=True),
                        waits=[("ve", vq), ("ve", self.pl_free)], inc="pe")
                    self.pl_free = s.op("vector", lambda q, g=g, m=m:
                                        q.tensor_reduce(
                                            s.NRM[0:1, 64 + g * 16 + m:
                                                  65 + g * 16 + m],
                                            s.PLp[0:1, :], AX.X, ALU.max),
                                        waits=[("pe", pe2)], inc="ve")
                hs = s.dma("sync", s.qsp[m * 128:(m + 1) * 128, :],
                           s.STG[:, stg * 1024:(stg + 1) * 1024],
                           waits=[("se", se)], inc="hsC%d" % stg)
                self.stg_free[stg] = ("hsC%d" % stg, hs)
            return se

        s.proj_feat(DC, st_q, 512, 4, lambda k, nn2: self.xlnt(k, nn2 * 512, 512),
                    epi_q)
        q_done_pe = s.n("pe")
        qsp_w = [("hsC0", s.n("hsC0")), ("hsC1", s.n("hsC1"))]
        qn_ve = s.n("ve")

        ngm = self.emit_negm(s.gkn_b, False, [("cg", ccn)], qn_ve)
        st_o = s.wstream_T(s.wsc["o"], 512, 4, self.pc["o"])
        borow, bosem, borow_gw = s.load_row("bo", s.wbq["bo"], D)
        att_ve = self.emit_attention(False, s.gk_b, s.gv_b, kw, vw,
                                     [], ngm, q_done_pe, qsp_w)
        rve = self.emit_out_proj(
            lambda k, t: s.AT[:, k * 1024 + t * 128: k * 1024 + (t + 1) * 128],
            st_o, borow, [(bosem, borow_gw)], extra_w=[("ve", att_ve)],
            ln_next=True)
        s.row_done("bo", "pe", s.n("pe"))
        return rve, s.n("pe")

    def emit_store(self, r_ve):
        st = self.dma("sync",
                      self.out_d[0:512, :].rearrange("(t p) d -> p t d", p=128),
                      self.R[:, 0:4 * D].rearrange("p (t d) -> p t d", d=D),
                      waits=[("ve", r_ve)], inc="hs")
        st2 = self.dma("scalar",
                       self.out_d[512:1024, :].rearrange("(t p) d -> p t d", p=128),
                       self.R[:, 4 * D:8 * D].rearrange("p (t d) -> p t d", d=D),
                       waits=[("ve", r_ve)], inc="haO")
        self.op("sync", lambda q: q.nop(), waits=[("hs", st), ("haO", st2)])

    def build(self):
        self.emit_init()
        r_ve = self.n("ve")
        sa_pe = 0
        if self.nsub >= 1:
            r_ve, sa_pe = self.emit_attn_a(
                r_ve, sa_pe, tile_w=lambda t: [("hsX%d" % (t // 2), 16)])
        if self.nsub >= 2:
            r_ve, sa_pe = self.emit_ffn(r_ve, sa_pe, ln_next=(self.nsub >= 3))
        if self.nsub >= 3:
            r_ve, sa_pe = self.emit_attn_b(r_ve, sa_pe)
        if self.nsub >= 4:
            r_ve, sa_pe = self.emit_ffn(r_ve, sa_pe)
        self.emit_store(r_ve)
        with self.nc.Block() as block:
            for e in ENGINES:
                stream = self.streams[e]
                if not stream:
                    continue

                chain_sem = {"vector": "ve", "scalar": "se"}.get(e)

                def make(stream=stream, chain_sem=chain_sem):
                    def body(q):
                        hi = {}
                        chain = 0
                        for fn, waits, inc, by in stream:
                            if chain_sem is not None and chain > 0:
                                if hi.get(chain_sem, -1) < chain:
                                    q.wait_ge(self.sems[chain_sem], chain)
                                    hi[chain_sem] = chain
                            for snm, v in waits:
                                if hi.get(snm, -1) < v:
                                    q.wait_ge(self.sems[snm], v)
                                    hi[snm] = v
                            ins = fn(q)
                            if inc is not None:
                                ins.then_inc(self.sems[inc], by)
                                if inc == chain_sem:
                                    chain += by
                    return body
                getattr(block, e)(make())
        return self.nc


_BUILD_CACHE = {}


def _get_nc(nsub=NSUB):
    if nsub not in _BUILD_CACHE:
        _BUILD_CACHE[nsub] = MK(nsub).build()
    return _BUILD_CACHE[nsub]


def _host_consts(w):
    pos = (4.0 * np.arange(T, dtype=np.float32) + np.float32(w))
    inv = np.exp(np.arange(0, DH, 2, dtype=np.float32)
                 * np.float32(-math.log(10000.0) / DH))[: RD // 2]
    ang = pos[:, None] * inv[None, :]
    cos = np.ascontiguousarray(np.tile(np.cos(ang).astype(np.float32).T, (4, 1)))
    sin = np.ascontiguousarray(np.tile(np.sin(ang).astype(np.float32).T, (4, 1)))
    ii = np.arange(128)
    masks = np.zeros((4 * 128, 128), dtype=np.float32)
    for r in range(4):
        m = (4 * ii[:, None] + r <= 4 * ii[None, :] + w)
        masks[r * 128:(r + 1) * 128, :] = m.astype(np.float32)
    return cos, sin, masks


_PERMQ = np.empty(H * RD, np.int64)
for _c in range(4):
    for _j in range(128):
        _h = 4 * _c + _j // 32
        _i2 = _j % 32
        _PERMQ[(2 * _c) * 128 + _j] = _h * RD + 2 * _i2
        _PERMQ[(2 * _c + 1) * 128 + _j] = _h * RD + 2 * _i2 + 1
_PERMK = np.concatenate([np.arange(0, RD, 2), np.arange(1, RD, 2)])


def kernel(**inputs):
    global _dbg
    nc = _get_nc(NSUB)
    x = np.asarray(inputs["L0_attn_a_x"])
    base = {}
    for k in ("w_dkv", "b_dkv", "w_dq", "b_dq", "w_rk", "b_rk", "w_rq", "b_rq",
              "w_uk", "b_uk", "w_uv", "b_uv", "w_uq", "b_uq", "w_o", "b_o"):
        v = np.asarray(inputs["L0_attn_a_" + k])
        base["a_" + k] = v.reshape(1, -1) if v.ndim == 1 else v
    for k, perm in (("w_rq", _PERMQ), ("b_rq", _PERMQ),
                    ("w_rk", _PERMK), ("b_rk", _PERMK)):
        base["a_" + k] = np.ascontiguousarray(base["a_" + k][..., perm])
    for k in ("Wq", "bq", "Wk", "bk", "Wv", "bv", "Wo", "bo"):
        v = np.asarray(inputs["L0_attn_b_" + k])
        base["b_" + k] = v.reshape(1, -1) if v.ndim == 1 else v
    base["f_w"] = np.asarray(inputs["L0_ffn_weight"])
    base["f_s"] = np.asarray(inputs["L0_ffn_scale_weight"]).reshape(1, -1)
    base["f_b"] = np.asarray(inputs["L0_ffn_bias"]).reshape(1, -1)

    in_maps = []
    for core in range(8):
        b, w = core // 4, core % 4
        cos, sin, masks = _host_consts(w)
        m = dict(base)
        m["x"] = np.ascontiguousarray(x[b, w::4, :])
        m["cos_tab"] = cos
        m["sin_tab"] = sin
        m["masks"] = masks
        in_maps.append(m)

    res = run_bass_kernel_spmd(nc, in_maps, core_ids=list(range(8)))
    out = np.empty((B, S, D), dtype=np.float32)
    for core in range(8):
        b, w = core // 4, core % 4
        out[b, w::4, :] = res.results[core]["out"]
    if DEBUG:
        kernel._dbg = res.results
    return out


# version 11 (final)
# speedup vs baseline: 1.0162x; 1.0162x over previous
import math
import numpy as np
import ml_dtypes
import concourse.bass as bass
import concourse.mybir as mybir
from concourse.bass_utils import run_bass_kernel_spmd

F32 = mybir.dt.float32
BF16 = mybir.dt.bfloat16
AF = mybir.ActivationFunctionType
ALU = mybir.AluOpType
AX = mybir.AxisListType
bf16 = ml_dtypes.bfloat16

B, S, D = 2, 4096, 2048
H, DH, RD = 16, 128, 64
KVD, QLD = 512, 1536
T, NT, DC = 1024, 8, 16
SA_A = 1.0 / math.sqrt(DH + RD)
SA_B = 1.0 / math.sqrt(math.sqrt(DH))
EPS = 1e-5
NSUB = 4
DEBUG = False

KRA = D + RD + 1
KRB = D + 1

ENGINES = ("sync", "scalar", "vector", "tensor", "gpsimd")
GROUPS = [[0, 1, 2, 3], [4, 5, 6, 7]]


def ts(i, w=128):
    return slice(i * w, (i + 1) * w)


class MK:
    def __init__(self, nsub=NSUB):
        self.nsub = nsub
        self.nc = bass.Bass("TRN2", target_bir_lowering=False)
        self.streams = {e: [] for e in ENGINES}
        self.cnt = {}
        self.sems = {}
        self._regpar = 0
        self.ctb_free = 0
        self.pending_xln = None
        self._declare_io()
        self._alloc()

    def _sem(self, name):
        if name not in self.sems:
            self.sems[name] = self.nc.alloc_semaphore(name=name)
            self.cnt[name] = 0
        return self.sems[name]

    def n(self, name):
        self._sem(name)
        return self.cnt[name]

    def op(self, eng, fn, waits=(), inc=None, by=1):
        waits = [(sn, v) for (sn, v) in waits if sn is not None and v and v > 0]
        for sn, _ in waits:
            self._sem(sn)
        if inc is not None:
            self._sem(inc)
            self.cnt[inc] += by
        self.streams[eng].append((fn, waits, inc, by))
        return self.cnt[inc] if inc is not None else None

    def dma(self, eng, out, in_, waits=(), inc=None, **kw):
        return self.op(eng, lambda q, o=out, i=in_, k=dict(kw): q.dma_start(out=o, in_=i, **k),
                       list(waits), inc, by=16)

    def _declare_io(self):
        nc = self.nc

        def inp(name, shape, dt=F32):
            return nc.dram_tensor(name, list(shape), dt, kind="ExternalInput").ap()

        self.x_in = inp("x", (T, D))
        self.wa = {k: inp("a_" + k, sh) for k, sh in {
            "w_dkv": (D, KVD), "b_dkv": (1, KVD), "w_dq": (D, QLD), "b_dq": (1, QLD),
            "w_rk": (D, RD), "b_rk": (1, RD), "w_rq": (QLD, H * RD), "b_rq": (1, H * RD),
            "w_uk": (KVD, D), "b_uk": (1, D), "w_uv": (KVD, D), "b_uv": (1, D),
            "w_uq": (QLD, D), "b_uq": (1, D), "w_o": (D, D), "b_o": (1, D)}.items()}
        self.wbq = {k: inp("b_" + k, sh) for k, sh in {
            "Wq": (D, D), "bq": (1, D), "Wk": (D, D), "bk": (1, D),
            "Wv": (D, D), "bv": (1, D), "Wo": (D, D), "bo": (1, D)}.items()}
        self.wf = {k: inp("f_" + k, sh) for k, sh in
                   {"w": (D, D), "s": (1, D), "b": (1, D)}.items()}
        self.cos_in = inp("cos_tab", (128, T))
        self.sin_in = inp("sin_tab", (128, T))
        self.msk_in = inp("masks", (4 * 128, 128))
        self.out_d = nc.dram_tensor("out", [T, D], F32, kind="ExternalOutput").ap()

        def scr(name, shape):
            return nc.dram_tensor(name, list(shape), BF16, kind="Internal").ap()

        self.wsc = {k: scr("wsc_" + k, (D, D)) for k in ("q", "k", "v", "o", "f")}
        self.wsa = {k: scr("wsa_" + k, sh) for k, sh in {
            "w_dkv": (D, KVD), "w_rk": (D, RD), "w_uk": (KVD, D),
            "w_uv": (KVD, D), "w_dq": (D, QLD), "w_rq": (QLD, H * RD),
            "w_uq": (QLD, D), "w_o": (D, D)}.items()}
        self.rbf_src = {
            "b_dkv": (self.wa["b_dkv"], KVD), "b_dq": (self.wa["b_dq"], QLD),
            "b_rk": (self.wa["b_rk"], RD), "b_rq": (self.wa["b_rq"], H * RD),
            "b_uk": (self.wa["b_uk"], D), "b_uv": (self.wa["b_uv"], D),
            "b_uq": (self.wa["b_uq"], D), "b_o": (self.wa["b_o"], D),
            "bq": (self.wbq["bq"], D), "bk": (self.wbq["bk"], D),
            "bv": (self.wbq["bv"], D), "bo": (self.wbq["bo"], D),
            "f_s": (self.wf["s"], D), "f_b": (self.wf["b"], D)}
        self.rbf = {k: scr("rbf_" + k, (1, n))
                    for k, (_, n) in self.rbf_src.items()}
        for ab in ("a", "b"):
            setattr(self, "ck_" + ab,
                    [scr("ck%s_c%d" % (ab, g), (1024, T)) for g in range(2)])
            setattr(self, "gk_" + ab,
                    [scr("gk%s_c%d" % (ab, g), (4096, T)) for g in range(2)])
            setattr(self, "cv_" + ab,
                    [scr("cv%s_c%d" % (ab, g), (1024, T)) for g in range(2)])
            setattr(self, "gv_" + ab,
                    [scr("gv%s_c%d" % (ab, g), (4096, T)) for g in range(2)])
        self.ckn_b = scr("ckn_b", (1, T))
        self.gkn_b = scr("gkn_b", (4, T))
        self.ckm_a = scr("ckm_a", (RD + 1, T))
        self.gkm_a = scr("gkm_a", (4 * (RD + 1), T))
        self.qsp = scr("qsp", (D + H * RD, T))

        self.ident_c = nc.inline_tensor(np.eye(128, dtype=bf16), "c_ident").ap()
        self.ones_r32_c = nc.inline_tensor(np.ones((1, 128), np.float32), "c_or32").ap()
        self.ones_rbf_c = nc.inline_tensor(np.ones((1, 128), bf16), "c_orbf").ap()
        self.ones_cbf_c = nc.inline_tensor(np.ones((128, 1), bf16), "c_ocbf").ap()
        self.one11_c = nc.inline_tensor(np.ones((1, 1), np.float32), "c_o11").ap()
        self.one11b_c = nc.inline_tensor(np.ones((1, 1), bf16), "c_o11b").ap()

    def _alloc(self):
        sb = self.nc.alloc_sbuf_tensor
        self.R = sb("R", [128, NT * D], F32)
        self.SAb = sb("SAb", [128, 16384], BF16)
        self.WB = sb("WB", [128, 16384], BF16)
        self.AT = sb("AT", [128, 16384], BF16)
        self.SC2 = sb("SC2", [128, 2048], BF16)
        self.SQB = sb("SQB", [128, 1024], BF16)
        self.LNS = sb("LNS", [128, 80], F32)
        self.BCOL = sb("BCOL", [128, 16], F32)
        self.ROW = sb("ROW", [1, 4096], BF16)
        self.XLNW = self.SC2.ap()
        self.MSK = sb("MSK", [128, 512], BF16)
        self.RKG = sb("RKG", [64, 4096], BF16)
        self.STG = sb("STG", [128, 2048], BF16)
        self.RQB = sb("RQB", [64, 2048], BF16)
        self.PBuf = sb("PBuf", [128, 2048], BF16)
        self.NRM = sb("NRM", [1, 384], F32)
        self.KROW = sb("KROW", [1, 1024], BF16)
        self.NEGM = sb("NEGM", [128, 32], F32)
        self.RLB = sb("RLB", [128, 512], F32)
        self.RLr = self.RLB.ap()[0:1, :]
        self.NRMB = sb("NRMB", [1, 128], BF16)
        self.IDN = sb("IDN", [128, 128], BF16)
        self.ONER32 = sb("ONER32", [1, 128], F32)
        self.ONERBF = sb("ONERBF", [1, 128], BF16)
        self.ONECBF = sb("ONECBF", [128, 1], BF16)
        self.ONE11 = sb("ONE11", [1, 1], F32)
        self.ONE11B = sb("ONE11B", [1, 1], BF16)
        self.CTB = sb("CTB", [128, 1024], F32)
        self.SSLB = sb("SSLB", [128, 512], BF16)
        self.TMPB = self.SQB.ap()[:, 0:1024].bitcast(F32)

        pa = self.nc.alloc_psum_tensor
        self.PS_A = pa("PS_A", [128, 1024], F32).ap()
        self.PS_B = pa("PS_B", [128, 1024], F32).ap()
        self.PO_A = pa("PO_A", [128, 512], F32).ap()
        self.PO_B = pa("PO_B", [128, 512], F32).ap()
        self.PLp = pa("PLp", [1, 512], F32).ap()
        self.PXp = pa("PXp", [128, 512], F32).ap()
        self.PXb = self.PXp.bitcast(BF16)
        self.PSLOT = [self.PS_A[:, 0:512], self.PS_A[:, 512:1024],
                      self.PS_B[:, 0:512], self.PS_B[:, 512:1024]]
        self.PPAIR = [self.PS_A, self.PS_B]
        self.PO = [self.PO_A, self.PO_B]

        self._rowpar = 0
        self.row_free = {0: (None, 0), 1: (None, 0)}
        self.row_cur = {}

    def load_row(self, name, src_ap, n):
        slot = self._rowpar % 2
        self._rowpar += 1
        sem = "gwR%d" % slot
        w = [self.row_free[slot]] if self.row_free[slot][0] else []
        gw = self.dma("sync", self.ROW[0:1, slot * 2048: slot * 2048 + n],
                      self.rbf[name][0:1, 0:n],
                      waits=w + [("gwB", self.gwb_all)], inc=sem)
        self.row_cur[name] = (slot, gw)

        def slicer(lo=0, nn=None, slot=slot, n=n):
            nn = n - lo if nn is None else nn
            return self.ROW[0:1, slot * 2048 + lo: slot * 2048 + lo + nn]
        return slicer, sem, gw

    def row_done(self, name, sem, cnt):
        slot, _ = self.row_cur[name]
        self.row_free[slot] = (sem, cnt)

    def xlnt(self, k, c0, nn):
        return self.SAb[:, k * 1024 + c0: k * 1024 + c0 + nn]

    def emit_init(self):
        s = self
        for c in range(4):
            s.dma("sync", s.R[:, c * 2 * D:(c + 1) * 2 * D].rearrange(
                "p (t d) -> p t d", d=D),
                s.x_in[c * 256:(c + 1) * 256, :].rearrange(
                    "(t p) d -> p t d", p=128), inc="hsX%d" % c)
        s.dma("gpsimd", s.MSK[:, :].rearrange("p (r q) -> p r q", q=128),
              s.msk_in.rearrange("(r p) q -> p r q", p=128), inc="gwI")
        s.dma("sync", s.IDN[:, :], s.ident_c, inc="hs")
        s.dma("sync", s.ONER32[:, :], s.ones_r32_c, inc="hs")
        s.dma("sync", s.ONERBF[:, :], s.ones_rbf_c, inc="hs")
        s.dma("sync", s.ONECBF[:, :], s.ones_cbf_c, inc="hs")
        s.dma("sync", s.ONE11[:, :], s.one11_c, inc="hs")
        s.dma("sync", s.ONE11B[:, :], s.one11b_c, inc="hs")
        self.init_hs = s.n("hs")
        self.msk_gw = s.n("gwI")
        s.op("vector", lambda q: q.memset(s.KROW[0:1, :], 0.0), inc="ve")

        self.pc = {}
        self.px_free = s.n("ve")
        self.stg_free = [(None, 0)] * 2
        self.pl_free = s.n("ve")
        for nm, (src, n) in self.rbf_src.items():
            s.dma("gpsimd", s.rbf[nm], src[0:1, 0:n], inc="gwB")
        self.gwb_all = s.n("gwB")
        self.ca = {}
        for nm in ("w_dkv", "w_rk", "w_uk", "w_uv", "w_dq", "w_rq",
                   "w_uq"):
            s.dma("gpsimd", s.wsa[nm], s.wa[nm],
                  waits=[("gwC", s.n("gwC"))], inc="gwC")
            self.ca[nm] = ("gwC", s.n("gwC"))

    def emit_casts(self):
        s = self
        for nm, src2 in [("f", s.wf["w"]), ("q", s.wbq["Wq"]),
                         ("k", s.wbq["Wk"]), ("v", s.wbq["Wv"]),
                         ("o", s.wbq["Wo"])]:
            s.dma("gpsimd", s.wsc[nm], src2, inc="gwP")
        pc_all = s.n("gwP")
        for nm in ("f", "q", "k", "v", "o"):
            self.pc[nm] = pc_all

    def bcol(self, name, src_ap, nchunks, scale=None):
        s = self
        rower, rsem, gw = s.load_row(name, src_ap, nchunks * 128)
        pe = None
        guard = s.n("se")
        for c in range(nchunks):
            pe = s.op("tensor", lambda q, c=c, rower=rower: q.matmul(
                s.PXp[:, c:c + 1], rower(c * 128, 128), s.ONE11B[:, :],
                start=True, stop=True),
                waits=[("hs", self.init_hs), (rsem, gw),
                       ("ve", self.px_free)], inc="pe")
        s.row_done(name, "pe", s.n("pe"))
        ve = s.op("vector", lambda q, nn=nchunks: q.tensor_copy(
            s.BCOL[:, 0:nn], s.PXp[:, 0:nn]),
            waits=[("pe", pe), ("se", guard)], inc="ve")
        if scale is not None:
            ve = s.op("vector", lambda q, nn=nchunks, sc=scale: q.tensor_scalar(
                s.BCOL[:, 0:nn], s.BCOL[:, 0:nn], sc, None, ALU.mult),
                waits=[("ve", ve)], inc="ve")
        self.px_free = ve
        return ve

    class WStream:

        def __init__(self, mk, loader, nslices, post=None):
            self.mk, self.loader, self.nslices = mk, loader, nslices
            self.post = post
            self.ld = {}

        def prefetch(self, si):
            if si >= self.nslices or si in self.ld:
                return
            mk = self.mk
            reg = 8192 * (mk._regpar % 2)
            mk._regpar += 1
            ld = self.loader(reg, si)
            if self.post is not None:
                ld = self.post(reg, si, ld)
            self.ld[si] = (reg, ld)

        def get(self, si):
            self.prefetch(si)
            return self.ld[si]

    def wstream_straight(self, src, nchunks, sw, nslices, cast_w=()):
        qsz = max(1, nchunks // 4)
        nq = (nchunks + qsz - 1) // qsz

        def loader(reg, si):
            pew = self.n("pe")
            cnts = []
            for qq in range(nq):
                sem = "gwW%d_%d" % (reg // 8192, qq)
                k0 = qq * qsz
                k1 = min(nchunks, k0 + qsz)
                ap = src[k0 * 128:k1 * 128, si * sw:(si + 1) * sw].rearrange(
                    "(k p) c -> p k c", p=128)
                cnts.append((sem, self.dma(
                    "sync",
                    self.WB[:, reg + k0 * sw:reg + k1 * sw].rearrange(
                        "p (k c) -> p k c", c=sw),
                    ap, waits=[("pe", pew)] + list(cast_w), inc=sem)))
            return (None, cnts, qsz)
        return MK.WStream(self, loader, nslices)

    @staticmethod
    def qgate(ld, k):
        if isinstance(ld[1], list):
            return ld[1][min(k // ld[2], len(ld[1]) - 1)]
        return (ld[0], ld[1])

    @staticmethod
    def qall(ld):
        if isinstance(ld[1], list):
            return list(ld[1])
        return [(ld[0], ld[1])]

    def wstream_T(self, src_bf, sw, nslices, pc_gw, post=None):
        def loader(reg, si):
            sem = "haW%d" % (reg // 8192)
            hac = None
            pew = self.n("pe")
            for i in range(16):
                hac = self.dma("sync", self.WB[:, reg + i * sw: reg + (i + 1) * sw],
                               src_bf[si * sw:(si + 1) * sw, ts(i)],
                               waits=[("pe", pew), ("gwP", pc_gw)], inc=sem,
                               transpose=True)
            return (sem, hac)
        return MK.WStream(self, loader, nslices, post=post)

    def ln_begin(self):
        self._ln_tp = self.n("pe")

    def emit_ln_tile(self, t, waits0, gate_pe, eng="vector"):
        s = self
        sm = "po" if eng == "gpsimd" else "ve"
        slots = {"S1": 0, "S2a": 8, "S2b": 16, "MEAN": 24, "MSQ": 32, "VAR": 40,
                 "LNV": 48, "RSTD": 56, "NMR": 64}

        def L(nm, t):
            return s.LNS[:, slots[nm] + t:slots[nm] + t + 1]

        if True:
            Rt = s.R[:, t * D:(t + 1) * D]
            s.op("vector", lambda q, Rt=Rt, t=t: q.tensor_reduce(
                L("S1", t), Rt, AX.X, ALU.add),
                waits=list(waits0), inc="ve")
            s.op("vector", lambda q, Rt=Rt, t=t: q.scalar_tensor_tensor(
                s.SQB[:, 0:1024], Rt[:, 0:1024], 1.0, Rt[:, 0:1024],
                ALU.mult, ALU.mult, accum_out=L("S2a", t)), inc="ve")
            sqv = s.op("vector", lambda q, Rt=Rt, t=t: q.scalar_tensor_tensor(
                s.SQB[:, 0:1024], Rt[:, 1024:2048], 1.0, Rt[:, 1024:2048],
                ALU.mult, ALU.mult, accum_out=L("S2b", t)), inc="ve")
            s.op(eng, lambda q, t=t: q.tensor_tensor(
                L("S2a", t), L("S2a", t), L("S2b", t), ALU.add),
                waits=[("ve", sqv)], inc=sm)
            s.op(eng, lambda q, t=t: q.tensor_scalar(
                L("MEAN", t), L("S1", t), 1.0 / D, None, ALU.mult), inc=sm)
            s.op(eng, lambda q, t=t: q.tensor_scalar(
                L("VAR", t), L("S2a", t), 1.0 / D, EPS, ALU.mult, ALU.add), inc=sm)
            s.op(eng, lambda q, t=t: q.tensor_tensor(
                L("MSQ", t), L("MEAN", t), L("MEAN", t), ALU.mult), inc=sm)
            v6 = s.op(eng, lambda q, t=t: q.tensor_tensor(
                L("VAR", t), L("VAR", t), L("MSQ", t), ALU.subtract), inc=sm)
            s.op("scalar", lambda q, t=t: q.activation(
                L("LNV", t), L("VAR", t), AF.Ln), waits=[(sm, v6)], inc="se")
            s2 = s.op("scalar", lambda q, t=t: q.activation(
                L("RSTD", t), L("LNV", t), AF.Exp, scale=-0.5), inc="se")
            v7 = s.op("vector", lambda q, t=t: q.scalar_tensor_tensor(
                L("NMR", t), L("MEAN", t), -1.0, L("RSTD", t), ALU.mult, ALU.mult),
                waits=[("se", s2)], inc="ve")
            v8 = s.op("vector", lambda q, Rt=Rt, t=t: q.tensor_scalar(
                s.XLNW[:, 0:2048], Rt, L("RSTD", t), L("NMR", t),
                ALU.mult, ALU.add),
                waits=[("ve", v7), ("pe", self._ln_tp)], inc="ve")
            pe_r = None
            for rnd in range(2):
                for j in range(8):
                    k = rnd * 8 + j
                    pe_r = s.op("tensor", lambda q, k=k, j=j: q.transpose(
                        s.PXb[:, ts(j)],
                        s.XLNW[:, k * 128:(k + 1) * 128],
                        s.IDN[:, :]),
                        waits=[("ve", v8), ("ve", self.px_free),
                               ("pe", gate_pe), ("hs", self.init_hs)], inc="pe")
                self.px_free = s.op("vector", lambda q, t=t, rnd=rnd: q.tensor_copy(
                    s.SAb[:, :].rearrange("p (k c) -> p k c", c=1024)[
                        :, rnd * 8:(rnd + 1) * 8, t * 128:(t + 1) * 128],
                    s.PXb[:, 0:1024].rearrange("p (j c) -> p j c", c=128)),
                    waits=[("pe", pe_r)], inc="ve")
            self._ln_tp = pe_r
        return s.n("ve")

    def emit_ln(self, r_ve, sa_free_pe=0, tile_w=None):
        self.ln_begin()
        for t in range(NT):
            tw = tile_w(t) if tile_w else [("hs", self.init_hs)]
            self.emit_ln_tile(t, [("ve", r_ve)] + tw, sa_free_pe)
        return self.n("ve")

    def proj_feat(self, nck, stream, sw, nslices, rhs_fn, epi_fn, extra_w=(),
                  init_free=None):
        s = self
        stream.prefetch(0)
        stream.prefetch(1)
        free = list(init_free) if init_free else [("se", s.n("se"))] * 4
        idx = 0
        mps = sw // 128
        for si in range(nslices):
            reg, ld = stream.get(si)
            for ml in range(mps):
                for nn2 in range(2):
                    slp = idx % 4
                    idx += 1
                    pe = None
                    for k in range(nck):
                        w = []
                        if k == 0:
                            w = [free[slp],
                                 ("hs", self.init_hs)] + list(extra_w)
                        if k == 0 or (isinstance(ld[1], list) and
                                      k % ld[2] == 0):
                            w.append(MK.qgate(ld, k))
                        pe = s.op("tensor", lambda q, reg=reg, k=k, ml=ml, nn2=nn2,
                                  slp=slp, fst=(k == 0), lst=(k == nck - 1), sw=sw:
                                  q.matmul(s.PSLOT[slp],
                                           s.WB[:, reg + k * sw + ml * 128:
                                                reg + k * sw + (ml + 1) * 128],
                                           rhs_fn(k, nn2), start=fst, stop=lst),
                                  waits=w, inc="pe" if k == nck - 1 else None)
                    free[slp] = ("se", epi_fn(si * mps + ml, nn2, slp, pe))
            stream.prefetch(si + 2)

    def proj_tok(self, nck, stream, sw, nslices, lhs_fn, rowfn, row_w, epi_fn,
                 extra_w=()):
        s = self
        stream.prefetch(0)
        stream.prefetch(1)
        free = [(None, 0)] * 4
        idx = 0
        for si in range(nslices):
            reg, ld = stream.get(si)
            for t in range(NT):
                slp = idx % 4
                idx += 1
                fw = [free[slp]] if free[slp][0] else []
                s.op("tensor", lambda q, slp=slp, rowfn=rowfn, si=si:
                     q.matmul(s.PSLOT[slp], s.ONERBF[:, :],
                              rowfn(si * 512, 512), start=True, stop=False),
                     waits=fw + [MK.qgate(ld, 0)] + list(row_w) +
                     [("hs", self.init_hs)] + list(extra_w), inc=None)
                pe = None
                for k in range(nck):
                    w = []
                    if isinstance(ld[1], list) and k % ld[2] == 0 and k > 0:
                        w = [MK.qgate(ld, k)]
                    pe = s.op("tensor", lambda q, reg=reg, k=k, t=t, slp=slp, sw=sw,
                              lst=(k == nck - 1):
                              q.matmul(s.PSLOT[slp], lhs_fn(k, t),
                                       s.WB[:, reg + k * sw: reg + (k + 1) * sw],
                                       start=False, stop=lst),
                              waits=w, inc="pe" if k == nck - 1 else None)
                free[slp] = epi_fn(si, t, slp, pe)
            stream.prefetch(si + 2)

    def emit_out_proj(self, lhs_fn, stream, rowfn, row_w, extra_w=(),
                      ln_next=False):
        s = self
        stream.prefetch(0)
        stream.prefetch(1)
        free = [("ve", 0)] * 4
        idx = 0
        rve = s.n("ve")
        for si in range(4):
            reg, ld = stream.get(si)
            for t in range(NT):
                slp = idx % 4
                idx += 1
                fw = [free[slp]] if free[slp][1] else []
                s.op("tensor", lambda q, slp=slp, si=si, rowfn=rowfn: q.matmul(
                    s.PSLOT[slp], s.ONERBF[:, :], rowfn(si * 512, 512),
                    start=True, stop=False),
                    waits=fw + [MK.qgate(ld, 0)] + list(row_w)
                    + list(extra_w), inc=None)
                pe = None
                for k in range(DC):
                    w = []
                    if isinstance(ld[1], list) and k % ld[2] == 0 and k > 0:
                        w = [MK.qgate(ld, k)]
                    pe = s.op("tensor", lambda q, reg=reg, k=k, t=t, slp=slp,
                              lst=(k == DC - 1):
                              q.matmul(s.PSLOT[slp], lhs_fn(k, t),
                                       s.WB[:, reg + k * 512: reg + (k + 1) * 512],
                                       start=False, stop=lst),
                              waits=w, inc="pe" if k == DC - 1 else None)
                rve = s.op("vector", lambda q, t=t, si=si, slp=slp:
                           q.tensor_tensor(
                               s.R[:, t * D + si * 512: t * D + (si + 1) * 512],
                               s.R[:, t * D + si * 512: t * D + (si + 1) * 512],
                               s.PSLOT[slp], ALU.add),
                           waits=[("pe", pe)], inc="ve")
                free[slp] = ("ve", rve)
                if ln_next and si == 3:
                    if t == 0:
                        self.ln_begin()
                    self.emit_ln_tile(t, [("ve", rve)], s.n("pe"),
                                      eng="gpsimd")
            stream.prefetch(si + 2)
        if ln_next:
            self.pending_xln = s.n("ve")
        return rve

    def head_norm(self, src_fn, out_col, kdim=128, extra_w=()):
        s = self
        ones = s.ONECBF[:, :] if kdim == 128 else s.ONECBF[0:kdim, :]
        mx = None
        for g in range(2):
            pe = s.op("tensor", lambda q, g=g, ones=ones: q.matmul(
                s.PLp[0:1, :], ones, src_fn(g), start=True, stop=True),
                waits=[("ve", self.pl_free)] + list(extra_w), inc="pe")
            mx = s.op("vector", lambda q, g=g: q.tensor_reduce(
                s.NRM[0:1, 340 + g:341 + g], s.PLp[0:1, :], AX.X, ALU.max),
                waits=[("pe", pe)], inc="ve")
            self.pl_free = mx
        return s.op("vector", lambda q, oc=out_col: q.tensor_tensor(
            s.NRM[0:1, oc:oc + 1], s.NRM[0:1, 340:341], s.NRM[0:1, 341:342],
            ALU.max), waits=[("ve", mx)], inc="ve")

    def emit_negm(self, row_fn, is_a, cc_w, qn_ve):
        s = self
        gw = None
        for r in range(4):
            gw = s.dma("scalar", s.NRMB[0:1, r * 32:(r + 1) * 32],
                       row_fn(r),
                       waits=list(cc_w), inc="gwN")
        cv = s.op("vector", lambda q: q.tensor_copy(
            s.NRM[0:1, 192:320], s.NRMB[0:1, :]),
            waits=[("gwN", gw)], inc="ve")
        ncol = 17 if is_a else 16
        v = s.op("vector", lambda q, n=ncol: q.tensor_tensor(
            s.NRM[0:1, 136:136 + n], s.NRM[0:1, 192:192 + n],
            s.NRM[0:1, 224:224 + n], ALU.max),
            waits=[("ve", cv), ("ve", qn_ve)], inc="ve")
        v = s.op("vector", lambda q, n=ncol: q.tensor_tensor(
            s.NRM[0:1, 136:136 + n], s.NRM[0:1, 136:136 + n],
            s.NRM[0:1, 256:256 + n], ALU.max), waits=[("ve", v)], inc="ve")
        v = s.op("vector", lambda q, n=ncol: q.tensor_tensor(
            s.NRM[0:1, 136:136 + n], s.NRM[0:1, 136:136 + n],
            s.NRM[0:1, 288:288 + n], ALU.max), waits=[("ve", v)], inc="ve")
        sv = s.op("scalar", lambda q, n=ncol: q.activation(
            s.NRM[0:1, 136:136 + n], s.NRM[0:1, 136:136 + n], AF.Ln),
            waits=[("ve", v)], inc="se")
        sv = s.op("scalar", lambda q, n=ncol: q.activation(
            s.NRM[0:1, 136:136 + n], s.NRM[0:1, 136:136 + n], AF.Exp, scale=0.5),
            waits=[("se", sv)], inc="se")
        qv = s.op("scalar", lambda q: q.activation(
            s.NRM[0:1, 64:96], s.NRM[0:1, 64:96], AF.Ln),
            waits=[("ve", qn_ve)], inc="se")
        qv = s.op("scalar", lambda q: q.activation(
            s.NRM[0:1, 64:96], s.NRM[0:1, 64:96], AF.Exp, scale=0.5),
            waits=[("se", qv)], inc="se")
        if is_a:
            rv = s.op("scalar", lambda q: q.activation(
                s.NRM[0:1, 34:35], s.NRM[0:1, 34:35], AF.Ln),
                waits=[("ve", qn_ve)], inc="se")
            rv = s.op("scalar", lambda q: q.activation(
                s.NRM[0:1, 34:35], s.NRM[0:1, 34:35], AF.Exp, scale=0.5),
                waits=[("se", rv)], inc="se")
            v = s.op("vector", lambda q: q.tensor_scalar(
                s.NRM[0:1, 136:152], s.NRM[0:1, 136:152], s.NRM[0:1, 152:153],
                None, ALU.add),
                waits=[("se", sv), ("se", rv), ("se", qv)], inc="ve")
            v = s.op("vector", lambda q: q.tensor_scalar(
                s.NRM[0:1, 64:96], s.NRM[0:1, 64:96], s.NRM[0:1, 34:35],
                None, ALU.add), waits=[("ve", v)], inc="ve")
        else:
            v = s.op("vector", lambda q: q.tensor_copy(
                s.NRM[0:1, 168:169], s.NRM[0:1, 136:137]),
                waits=[("se", sv), ("se", qv)], inc="ve")
        for g in range(2):
            v = s.op("vector", lambda q, g=g: q.tensor_tensor(
                s.NRM[0:1, 96 + g * 16:112 + g * 16],
                s.NRM[0:1, 64 + g * 16:80 + g * 16],
                s.NRM[0:1, 136:152], ALU.mult), waits=[("ve", v)], inc="ve")
        v = s.op("vector", lambda q: q.tensor_scalar(
            s.NRM[0:1, 96:128], s.NRM[0:1, 96:128], -1.05, None, ALU.mult),
            waits=[("ve", v)], inc="ve")
        pe = s.op("tensor", lambda q: q.matmul(
            s.PXp[:, 0:32], s.ONER32[:, :], s.NRM[0:1, 96:128], start=True,
            stop=True), waits=[("ve", v), ("ve", self.px_free)], inc="pe")
        self.px_free = s.op("vector", lambda q: q.tensor_copy(
            s.NEGM[:, :], s.PXp[:, 0:32]), waits=[("pe", pe)], inc="ve")
        return s.op("vector", lambda q: q.memset(s.NRM[0:1, 342:343], 0.0),
            waits=[("ve", self.px_free)], inc="ngm")

    def emit_attention(self, is_a, gkc, gvc, kw, vw, rw, ngm,
                       sa_free_pe, qsp_w):
        s = self
        kbuf = [s.SAb[:, 0:4096], s.SAb[:, 4096:8192]]
        vbuf = [s.SAb[:, 8192:12288], s.SAb[:, 12288:16384]]
        rkg_hs = 0
        if is_a:
            for r in range(4):
                rkg_hs = s.dma("sync", s.RKG[0:RD, r * 1024:(r + 1) * 1024],
                               s.gkm_a[r * (RD + 1):r * (RD + 1) + RD, :],
                               waits=list(rw) + [("pe", sa_free_pe)], inc="hsG")
        k_ld = [0, 0]
        v_ld = [0, 0]
        q_ld = [0, 0]
        slot_pe = [sa_free_pe, sa_free_pe]
        pb_free = [s.n("pe"), s.n("pe")]
        ps_free = [s.n("se"), s.n("se")]
        po_free = [s.n("ve"), s.n("ve")]
        attnt_pe = sa_free_pe
        gpair = [0]
        pending = [None]

        def load_head(h):
            sl = h % 2
            hg, hh = h // 8, h % 8
            for r in range(4):
                k_ld[sl] = s.dma("sync", kbuf[sl][:, r * 1024:(r + 1) * 1024],
                                 gkc[hg][r * 1024 + hh * 128:
                                         r * 1024 + (hh + 1) * 128, :],
                                 waits=[kw[h // 4], ("pe", slot_pe[sl])],
                                 inc="hsA%d" % sl)
                v_ld[sl] = s.dma("scalar",
                                 vbuf[sl][:, r * 1024:(r + 1) * 1024],
                                 gvc[hg][r * 1024 + hh * 128:
                                         r * 1024 + (hh + 1) * 128, :],
                                 waits=[vw[h // 4], ("pe", slot_pe[sl])],
                                 inc="haA%d" % sl)
            q_ld[sl] = s.dma("scalar", s.AT[:, h * 1024:(h + 1) * 1024],
                             s.qsp[h * 128:(h + 1) * 128, :],
                             waits=[("pe", slot_pe[sl])] + qsp_w,
                             inc="haA%d" % sl)
            if is_a:
                g2 = 2 * (h // 4)
                ro = (h % 4) * 32
                s.dma("scalar", s.RQB[0:32, sl * 1024:(sl + 1) * 1024],
                      s.qsp[D + g2 * 128 + ro: D + g2 * 128 + ro + 32, :],
                      waits=[("pe", slot_pe[sl])] + qsp_w, inc="haA%d" % sl)
                q_ld[sl] = s.dma("scalar",
                                 s.RQB[32:64, sl * 1024:(sl + 1) * 1024],
                                 s.qsp[D + (g2 + 1) * 128 + ro:
                                       D + (g2 + 1) * 128 + ro + 32, :],
                                 waits=[("pe", slot_pe[sl])] + qsp_w,
                                 inc="haA%d" % sl)

        def flush():
            if pending[0] is None:
                return
            (kcl, pbsl, qoff, se_e, ve_m, sl, po_sl, firsts, lasts, h, g) = pending[0]
            pending[0] = None
            pe_c = None
            for idx, (uu, rr) in enumerate(kcl):
                pslice = s.PBuf[:, pbsl * 1024 + idx * 512 + qoff:
                                pbsl * 1024 + (idx + 1) * 512]
                w = [("se", se_e)] + ([("ve", ve_m)] if ve_m else [])
                if firsts[idx]:
                    w += [("ve", self.pl_free), ("ve", po_free[po_sl]),
                          ("ngm", ngm)]
                s.op("tensor", lambda q, pslice=pslice, qoff=qoff,
                     fs=firsts[idx], lt=lasts[idx]:
                     q.matmul(s.PLp[0:1, qoff:512], s.ONECBF[:, :], pslice,
                              start=fs, stop=lt), waits=w, inc=None)
                vcol = (rr * 8 + uu) * 128
                pe_c = s.op("tensor", lambda q, pslice=pslice, vcol=vcol, sl=sl,
                            po_sl=po_sl, qoff=qoff, fs=firsts[idx], lt=lasts[idx]:
                            q.matmul(s.PO[po_sl][:, qoff:512],
                                     vbuf[sl][:, vcol:vcol + 128], pslice,
                                     start=fs, stop=lt),
                            waits=[("haA%d" % sl, v_ld[sl])], inc="pe")
            pb_free[pbsl] = pe_c
            if lasts[-1]:
                ve_r = s.op("vector", lambda q: q.reciprocal(
                    s.RLr[0:1, :], s.PLp[0:1, :]), waits=[("pe", pe_c)], inc="ve")
                self.pl_free = ve_r
                pe_b = s.op("tensor", lambda q: q.matmul(
                    s.PXp[:, 0:512], s.ONER32[:, :], s.RLr[0:1, :],
                    start=True, stop=True),
                    waits=[("ve", ve_r), ("ve", self.px_free)], inc="pe")
                ve_c = s.op("vector", lambda q: q.tensor_copy(
                    s.RLB[:, :], s.PXp[:, 0:512]), waits=[("pe", pe_b)], inc="ve")
                self.px_free = ve_c
                ve_at = s.op("vector", lambda q, h=h, g=g, po_sl=po_sl:
                             q.tensor_tensor(
                                 s.AT[:, h * 1024 + g * 512: h * 1024 + (g + 1) * 512],
                                 s.PO[po_sl], s.RLB[:, :], ALU.mult),
                             waits=[("ve", ve_c), ("pe", attnt_pe)], inc="ve")
                po_free[po_sl] = ve_at

        load_head(0)
        load_head(1)
        for h in range(H):
            sl = h % 2
            for g in range(2):
                po_sl = (h * 2 + g) % 2
                kcs = [(u, r) for u in range(4 * g + 4) for r in range(4)]
                pairs = [(kcs[i], kcs[i + 1]) for i in range(0, len(kcs), 2)]
                npair = len(pairs)
                for pi, (kc0, kc1) in enumerate(pairs):
                    par = gpair[0] % 2
                    pbsl = gpair[0] % 2
                    gpair[0] += 1
                    u, _ = kc0
                    t0 = max(4 * g, u)
                    qoff = (t0 - 4 * g) * 128
                    pe_sc = None
                    for half, (uu, rr) in enumerate((kc0, kc1)):
                        kcol = rr * 1024 + uu * 128
                        dst = s.PPAIR[par][:, half * 512 + qoff:(half + 1) * 512]
                        qs = s.AT[:, h * 1024 + g * 512 + qoff:
                                  h * 1024 + (g + 1) * 512]
                        w0 = [("hsA%d" % sl, k_ld[sl]), ("haA%d" % sl, q_ld[sl]),
                              ("se", ps_free[par])]
                        if is_a:
                            s.op("tensor", lambda q, dst=dst, kcol=kcol, qs=qs,
                                 sl=sl: q.matmul(dst, kbuf[sl][:, kcol:kcol + 128],
                                                 qs, start=True, stop=False),
                                 waits=w0, inc=None)
                            rqs = s.RQB[0:RD, sl * 1024 + g * 512 + qoff:
                                        sl * 1024 + (g + 1) * 512]
                            pe_sc = s.op("tensor", lambda q, dst=dst, rqs=rqs,
                                         rr=rr, uu=uu:
                                         q.matmul(dst,
                                                  s.RKG[0:RD, rr * 1024 + uu * 128:
                                                        rr * 1024 + (uu + 1) * 128],
                                                  rqs, start=False, stop=True),
                                         waits=[("hsG", rkg_hs)],
                                         inc="pe" if half == 1 else None)
                        else:
                            pe_sc = s.op("tensor", lambda q, dst=dst, kcol=kcol,
                                         qs=qs, sl=sl:
                                         q.matmul(dst, kbuf[sl][:, kcol:kcol + 128],
                                                  qs, start=True, stop=True),
                                         waits=w0, inc="pe" if half == 1 else None)
                    flush()
                    negcol = s.NEGM[:, g * 16 + h: g * 16 + h + 1]
                    srcp = s.PPAIR[par].rearrange("p (hf c) -> p hf c", hf=2)[
                        :, :, qoff:512]
                    dstp = s.PBuf[:, pbsl * 1024:(pbsl + 1) * 1024].rearrange(
                        "p (hf c) -> p hf c", hf=2)[:, :, qoff:512]
                    se_e = s.op("scalar", lambda q, srcp=srcp, dstp=dstp,
                                negcol=negcol:
                                q.activation(dstp, srcp, AF.Exp, bias=negcol),
                                waits=[("pe", pe_sc), ("ngm", ngm),
                                       ("pe", pb_free[pbsl])], inc="se")
                    ps_free[par] = se_e
                    ve_m = None
                    if u >= 4 * g:
                        for half, (uu, rr) in enumerate((kc0, kc1)):
                            psl = s.PBuf[:, pbsl * 1024 + half * 512 + qoff:
                                         pbsl * 1024 + half * 512 + qoff + 128]
                            ve_m = s.op("vector", lambda q, psl=psl, rr=rr:
                                        q.tensor_tensor(
                                            psl, psl,
                                            s.MSK[:, rr * 128:(rr + 1) * 128],
                                            ALU.mult),
                                        waits=[("se", se_e), ("gwI", self.msk_gw)],
                                        inc="ve")
                    firsts = [pi == 0, False]
                    lasts = [False, pi == npair - 1]
                    pending[0] = ((kc0, kc1), pbsl, qoff, se_e, ve_m, sl, po_sl,
                                  firsts, lasts, h, g)
            if h + 2 < H:
                flush()
                slot_pe[sl] = s.n("pe")
                load_head(h + 2)
        flush()
        return s.n("ve")

    def emit_attn_a(self, r_ve, sa_free_pe, tile_w=None):
        s = self
        xln_ve = s.emit_ln(r_ve, sa_free_pe, tile_w)

        bc = s.bcol("b_dkv", s.wa["b_dkv"], 4)
        st_dkv = s.wstream_straight(s.wsa["w_dkv"], DC, 512, 1, [s.ca["w_dkv"]])

        def epi_lkv(m, nn2, slp, pe, bc=bc):
            return s.op("scalar", lambda q, m=m, nn2=nn2, slp=slp: q.activation(
                s.AT[:, 12288 + m * 1024 + nn2 * 512: 12288 + m * 1024 + (nn2 + 1) * 512],
                s.PSLOT[slp], AF.Identity, bias=s.BCOL[:, m:m + 1]),
                waits=[("pe", pe), ("ve", bc)], inc="se")

        s.proj_feat(DC, st_dkv, 512, 1, lambda k, nn2: self.xlnt(k, nn2 * 512, 512),
                    epi_lkv, extra_w=[("ve", xln_ve)])
        lkv_pe, lkv_se = s.n("pe"), s.n("se")

        def lkvt(k, c0, nn):
            return s.AT[:, 12288 + k * 1024 + c0: 12288 + k * 1024 + c0 + nn]

        def lqt(k, c0, nn):
            return s.AT[:, k * 1024 + c0: k * 1024 + c0 + nn]

        st_rk = s.wstream_straight(s.wsa["w_rk"], DC, RD, 1, [s.ca["w_rk"]])
        st_rk.prefetch(0)
        reg_rk, ld_rk = st_rk.get(0)
        rkrow, rksem, rkrow_gw = s.load_row("b_rk", s.wa["b_rk"], RD)
        pe = None
        for half in range(2):
            pe = s.op("tensor", lambda q, half=half: q.matmul(
                s.PXp[0:32, 30 + half:31 + half], rkrow(half * 32, 32),
                s.ONE11B[:, :], start=True, stop=True),
                waits=[("hs", self.init_hs), (rksem, rkrow_gw),
                       ("ve", self.px_free)], inc="pe")
        bk_ve = s.op("vector", lambda q: q.tensor_copy(
            s.BCOL[0:32, 14:16], s.PXp[0:32, 30:32]), waits=[("pe", pe)], inc="ve")
        self.px_free = bk_ve
        s.row_done("b_rk", "pe", s.n("pe"))
        s.op("vector", lambda q: q.memset(s.NRM[0:1, 16:17], 0.0), inc="ve")
        rk_free = [("se", lkv_se), ("se", lkv_se)]
        sq_pe = s.n("pe")
        for nn2 in range(2):
            ctw = [("ve", self.ctb_free)] if self.ctb_free else []
            s.dma("sync", s.CTB[:, 0:512],
                  s.cos_in[:, nn2 * 512:(nn2 + 1) * 512], waits=ctw, inc="hsT")
            s.dma("sync", s.CTB[:, 512:1024],
                  s.sin_in[:, nn2 * 512:(nn2 + 1) * 512], waits=ctw, inc="hsT")
            tw = s.n("hsT")
            pes = [None, None]
            for half in range(2):
                for k in range(DC):
                    w = []
                    if k == 0:
                        w = [rk_free[half], ("ve", xln_ve)]
                    if k % 4 == 0:
                        w.append(MK.qgate(ld_rk, k))
                    pes[half] = s.op("tensor", lambda q, k=k, half=half, nn2=nn2,
                                     fst=(k == 0), lst=(k == DC - 1), reg=reg_rk:
                                     q.matmul(s.PSLOT[half][0:32, :],
                                              s.WB[:, reg + k * RD + half * 32:
                                                   reg + k * RD + (half + 1) * 32],
                                              self.xlnt(k, nn2 * 512, 512),
                                              start=fst, stop=lst),
                                     waits=w, inc="pe" if k == DC - 1 else None)
            fw = [self.stg_free[0]] if self.stg_free[0][0] else []
            v1 = s.op("vector", lambda q: q.scalar_tensor_tensor(
                s.RLB[0:32, :], s.PSLOT[0][0:32, :], s.BCOL[0:32, 14:15],
                s.CTB[0:32, 0:512], ALU.add, ALU.mult),
                waits=[("pe", pes[0]), ("pe", pes[1]), ("ve", bk_ve),
                       ("hsT", tw)], inc="ve")
            v2 = s.op("vector", lambda q: q.scalar_tensor_tensor(
                s.TMPB[0:32, :], s.PSLOT[1][0:32, :], s.BCOL[0:32, 15:16],
                s.CTB[0:32, 512:1024], ALU.add, ALU.mult),
                waits=[("pe", sq_pe)], inc="ve")
            v3 = s.op("vector", lambda q: q.tensor_tensor(
                s.STG[0:32, 0:512], s.RLB[0:32, :], s.TMPB[0:32, :],
                ALU.subtract), waits=fw, inc="ve")
            v4 = s.op("vector", lambda q: q.scalar_tensor_tensor(
                s.RLB[0:32, :], s.PSLOT[0][0:32, :], s.BCOL[0:32, 14:15],
                s.CTB[0:32, 512:1024], ALU.add, ALU.mult), inc="ve")
            v5 = s.op("vector", lambda q: q.scalar_tensor_tensor(
                s.TMPB[0:32, :], s.PSLOT[1][0:32, :], s.BCOL[0:32, 15:16],
                s.CTB[0:32, 0:512], ALU.add, ALU.mult), inc="ve")
            v6 = s.op("vector", lambda q: q.tensor_tensor(
                s.STG[0:32, 512:1024], s.RLB[0:32, :], s.TMPB[0:32, :],
                ALU.add), inc="ve")
            rk_free = [("ve", v4), ("ve", v5)]
            self.ctb_free = v5
            hs = s.dma("sync", s.ckm_a[0:32, nn2 * 512:(nn2 + 1) * 512],
                       s.STG[0:32, 0:512], waits=[("ve", v3)], inc="hsC0")
            hs = s.dma("sync", s.ckm_a[32:RD, nn2 * 512:(nn2 + 1) * 512],
                       s.STG[0:32, 512:1024], waits=[("ve", v6)], inc="hsC0")
            self.stg_free[0] = ("hsC0", hs)
            vsq = s.op("vector", lambda q: q.scalar_tensor_tensor(
                s.SQB[0:32, 0:1024], s.STG[0:32, 0:1024], 1.0,
                s.STG[0:32, 0:1024], ALU.mult, ALU.mult),
                waits=[("ve", v6)], inc="ve")
            pe2 = None
            for g in range(2):
                pe2 = s.op("tensor", lambda q, g=g: q.matmul(
                    s.PLp[0:1, :], s.ONECBF[0:32, :],
                    s.SQB[0:32, g * 512:(g + 1) * 512],
                    start=(g == 0), stop=(g == 1)),
                    waits=[("ve", vsq), ("ve", self.pl_free)],
                    inc="pe" if g == 1 else None)
            sq_pe = pe2
            rv = s.op("vector", lambda q: q.tensor_reduce(
                s.NRM[0:1, 343:344], s.PLp[0:1, :], AX.X, ALU.max),
                waits=[("pe", pe2)], inc="ve")
            self.pl_free = s.op("vector", lambda q: q.tensor_tensor(
                s.NRM[0:1, 16:17], s.NRM[0:1, 16:17], s.NRM[0:1, 343:344],
                ALU.max), waits=[("ve", rv)], inc="ve")
        rk_sq_pe = sq_pe
        cw_r = [("hsC0", s.n("hsC0"))]

        bc = s.bcol("b_uk", s.wa["b_uk"], DC)
        st_uk = s.wstream_straight(s.wsa["w_uk"], 4, 512, 4, [s.ca["w_uk"]])
        cwk = {}

        def epi_k(m, nn2, slp, pe, bc=bc):
            stg = m % 2
            fw = [self.stg_free[stg]] if self.stg_free[stg][0] else []
            se = s.op("scalar", lambda q, nn2=nn2, slp=slp, stg=stg, m=m:
                      q.activation(
                          s.STG[:, stg * 1024 + nn2 * 512: stg * 1024 + (nn2 + 1) * 512],
                          s.PSLOT[slp], AF.Identity, bias=s.BCOL[:, m:m + 1]),
                      waits=[("pe", pe), ("ve", bc)] + fw, inc="se")
            if nn2 == 1:
                vq = s.op("vector", lambda q, stg=stg: q.scalar_tensor_tensor(
                    s.SQB[:, 0:1024], s.STG[:, stg * 1024:(stg + 1) * 1024], 1.0,
                    s.STG[:, stg * 1024:(stg + 1) * 1024], ALU.mult, ALU.mult),
                    waits=[("se", se)], inc="ve")
                self.head_norm(lambda g: s.SQB[:, g * 512:(g + 1) * 512], 17 + m,
                               extra_w=[("ve", vq)])
                hs = s.dma("sync", s.ck_a[m // 8][(m % 8) * 128:
                                                  (m % 8 + 1) * 128, :],
                           s.STG[:, stg * 1024:(stg + 1) * 1024],
                           waits=[("se", se)], inc="hsC%d" % stg)
                self.stg_free[stg] = ("hsC%d" % stg, hs)
                if m % 8 == 7:
                    cwk[m // 8] = [("hsC0", s.n("hsC0")),
                                   ("hsC1", s.n("hsC1"))]
            return se

        s.proj_feat(4, st_uk, 512, 4, lambda k, nn2: lkvt(k, nn2 * 512, 512),
                    epi_k, extra_w=[("pe", lkv_pe), ("se", lkv_se)],
                    init_free=[rk_free[0], rk_free[1],
                               ("se", lkv_se), ("se", lkv_se)])
        nv = s.op("vector", lambda q: q.tensor_copy(
            s.KROW[0:1, 0:16], s.NRM[0:1, 17:33]),
            waits=[("ve", s.n("ve"))], inc="ve")
        nv = s.op("vector", lambda q: q.tensor_copy(
            s.KROW[0:1, 16:17], s.NRM[0:1, 16:17]), waits=[("ve", nv)], inc="ve")
        s.dma("sync", s.ckm_a[RD:RD + 1, :], s.KROW[0:1, :],
              waits=[("ve", nv)], inc="hsK")
        cw_n = [("hsK", s.n("hsK"))]
        cck = {}
        for g in range(2):
            cck[g] = s.op("gpsimd", lambda q, g=g: q.collective_compute(
                "AllGather", ALU.bypass, replica_groups=GROUPS,
                ins=[s.ck_a[g]], outs=[s.gk_a[g]]), waits=cwk[g], inc="cg")
        ccn = s.op("gpsimd", lambda q: q.collective_compute(
            "AllGather", ALU.bypass, replica_groups=GROUPS,
            ins=[s.ckm_a], outs=[s.gkm_a]), waits=cw_r + cw_n, inc="cg")
        ccr = ccn
        s.dma("gpsimd", s.wsa["w_o"], s.wa["w_o"],
              waits=[("gwC", s.n("gwC"))], inc="gwC")
        s.ca["w_o"] = ("gwC", s.n("gwC"))

        st_uv = s.wstream_straight(s.wsa["w_uv"], 4, 512, 4, [s.ca["w_uv"]])
        uvrow, uvsem, uvrow_gw = s.load_row("b_uv", s.wa["b_uv"], D)
        cwv = {}

        def epi_v(si, t, slp, pe):
            stg = t % 2
            fw = [self.stg_free[stg]] if self.stg_free[stg][0] else []
            se = s.op("scalar", lambda q, slp=slp, stg=stg: q.activation(
                s.STG[:, stg * 1024: stg * 1024 + 512], s.PSLOT[slp], AF.Identity),
                waits=[("pe", pe)] + fw, inc="se")
            hs = s.dma("sync",
                       s.cv_a[si // 2][(si % 2) * 512:(si % 2 + 1) * 512,
                                       t * 128:(t + 1) * 128]
                       .rearrange("(h p) d -> p h d", p=128),
                       s.STG[:, stg * 1024: stg * 1024 + 512]
                       .rearrange("p (h d) -> p h d", d=128),
                       waits=[("se", se)], inc="hsC%d" % stg)
            self.stg_free[stg] = ("hsC%d" % stg, hs)
            if t == NT - 1 and si % 2 == 1:
                cwv[si // 2] = [("hsC0", s.n("hsC0")), ("hsC1", s.n("hsC1"))]
            return ("se", se)

        s.proj_tok(4, st_uv, 512, 4, lambda k, t: lkvt(k, t * 128, 128),
                   uvrow, [(uvsem, uvrow_gw)], epi_v, extra_w=[("se", lkv_se)])
        s.row_done("b_uv", "pe", s.n("pe"))
        ccv = {}
        for g in range(2):
            ccv[g] = s.op("gpsimd", lambda q, g=g: q.collective_compute(
                "AllGather", ALU.bypass, replica_groups=GROUPS,
                ins=[s.cv_a[g]], outs=[s.gv_a[g]]), waits=cwv[g], inc="cg")
        kw = [("cg", cck[g // 2]) for g in range(4)]
        vw = [("cg", ccv[g // 2]) for g in range(4)]
        self.emit_casts()

        bc = s.bcol("b_dq", s.wa["b_dq"], 12)
        st_dq = s.wstream_straight(s.wsa["w_dq"], DC, 512, 3, [s.ca["w_dq"]])

        def epi_lq(m, nn2, slp, pe, bc=bc):
            return s.op("scalar", lambda q, m=m, nn2=nn2, slp=slp: q.activation(
                s.AT[:, m * 1024 + nn2 * 512: m * 1024 + (nn2 + 1) * 512],
                s.PSLOT[slp], AF.Identity, bias=s.BCOL[:, m:m + 1]),
                waits=[("pe", pe), ("ve", bc)], inc="se")

        s.proj_feat(DC, st_dq, 512, 3, lambda k, nn2: self.xlnt(k, nn2 * 512, 512),
                    epi_lq)
        lq_pe = s.n("pe")

        lq_se0 = s.n("se")
        st_rq = s.wstream_straight(s.wsa["w_rq"], 12, 512, 2, [s.ca["w_rq"]])
        st_rq.prefetch(0)
        st_rq.prefetch(1)
        bcq = s.bcol("b_rq", s.wa["b_rq"], 8, scale=SA_A)
        scl = {}
        for si in range(2):
            reg, ld = st_rq.get(si)
            scl[si] = s.op("vector", lambda q, reg=reg: q.tensor_scalar(
                s.WB[:, reg:reg + 12 * 512], s.WB[:, reg:reg + 12 * 512],
                SA_A, None, ALU.mult), waits=MK.qall(ld), inc="ve")
        rqm_ve = s.op("vector", lambda q: q.memset(s.NRM[0:1, 34:35], 0.0),
                      inc="ve")
        sfree = [("se", lq_se0)] * 4
        sq_pe = s.n("pe")
        i = 0
        for nn2 in range(2):
            ctw = [("ve", self.ctb_free)] if self.ctb_free else []
            s.dma("sync", s.CTB[:, 0:512],
                  s.cos_in[:, nn2 * 512:(nn2 + 1) * 512], waits=ctw, inc="hsT")
            s.dma("sync", s.CTB[:, 512:1024],
                  s.sin_in[:, nn2 * 512:(nn2 + 1) * 512], waits=ctw, inc="hsT")
            tw = s.n("hsT")
            for si in range(2):
                reg, _ = st_rq.get(si)
                for pr in range(2):
                    c = si * 2 + pr
                    sE, sO = (2 * i) % 4, (2 * i + 1) % 4
                    i += 1
                    pes = {}
                    for half, slp in ((0, sE), (1, sO)):
                        ml = 2 * pr + half
                        pe = None
                        for k in range(12):
                            w = []
                            if k == 0:
                                w = [sfree[slp], ("ve", scl[si])]
                            pe = s.op("tensor", lambda q, k=k, ml=ml, slp=slp,
                                      reg=reg, nn2=nn2, fst=(k == 0),
                                      lst=(k == 11):
                                      q.matmul(s.PSLOT[slp],
                                               s.WB[:, reg + k * 512 + ml * 128:
                                                    reg + k * 512 + (ml + 1) * 128],
                                               lqt(k, nn2 * 512, 512),
                                               start=fst, stop=lst),
                                      waits=w, inc="pe" if k == 11 else None)
                        pes[half] = pe
                    stg = i % 2
                    st0 = stg * 1024
                    fw = [self.stg_free[stg]] if self.stg_free[stg][0] else []
                    bE = 2 * c
                    v1 = s.op("vector", lambda q, sE=sE, bE=bE:
                              q.scalar_tensor_tensor(
                                  s.RLB[:, :], s.PSLOT[sE], s.BCOL[:, bE:bE + 1],
                                  s.CTB[:, 0:512], ALU.add, ALU.mult),
                              waits=[("pe", pes[0]), ("pe", pes[1]), ("ve", bcq),
                                     ("hsT", tw)], inc="ve")
                    v2 = s.op("vector", lambda q, sO=sO, bE=bE:
                              q.scalar_tensor_tensor(
                                  s.TMPB[:, :], s.PSLOT[sO],
                                  s.BCOL[:, bE + 1:bE + 2],
                                  s.CTB[:, 512:1024], ALU.add, ALU.mult),
                              waits=[("pe", sq_pe)], inc="ve")
                    v3 = s.op("vector", lambda q, st0=st0: q.tensor_tensor(
                        s.STG[:, st0:st0 + 512], s.RLB[:, :], s.TMPB[:, :],
                        ALU.subtract), waits=fw, inc="ve")
                    v4 = s.op("vector", lambda q, sE=sE, bE=bE:
                              q.scalar_tensor_tensor(
                                  s.RLB[:, :], s.PSLOT[sE], s.BCOL[:, bE:bE + 1],
                                  s.CTB[:, 512:1024], ALU.add, ALU.mult), inc="ve")
                    v5 = s.op("vector", lambda q, sO=sO, bE=bE:
                              q.scalar_tensor_tensor(
                                  s.TMPB[:, :], s.PSLOT[sO],
                                  s.BCOL[:, bE + 1:bE + 2],
                                  s.CTB[:, 0:512], ALU.add, ALU.mult), inc="ve")
                    v6 = s.op("vector", lambda q, st0=st0: q.tensor_tensor(
                        s.STG[:, st0 + 512:st0 + 1024], s.RLB[:, :], s.TMPB[:, :],
                        ALU.add), inc="ve")
                    sfree[sE] = ("ve", v4)
                    sfree[sO] = ("ve", v5)
                    self.ctb_free = v5
                    hs = s.dma("sync",
                               s.qsp[D + 2 * c * 128:D + (2 * c + 1) * 128,
                                     nn2 * 512:(nn2 + 1) * 512],
                               s.STG[:, st0:st0 + 512], waits=[("ve", v3)],
                               inc="hsC%d" % stg)
                    hs = s.dma("sync",
                               s.qsp[D + (2 * c + 1) * 128:D + (2 * c + 2) * 128,
                                     nn2 * 512:(nn2 + 1) * 512],
                               s.STG[:, st0 + 512:st0 + 1024], waits=[("ve", v6)],
                               inc="hsC%d" % stg)
                    self.stg_free[stg] = ("hsC%d" % stg, hs)
                    vsq = s.op("vector", lambda q, st0=st0:
                               q.scalar_tensor_tensor(
                                   s.SQB[:, 0:1024], s.STG[:, st0:st0 + 1024], 1.0,
                                   s.STG[:, st0:st0 + 1024], ALU.mult, ALU.mult),
                               waits=[("ve", v6)], inc="ve")
                    pe2 = None
                    for g in range(2):
                        pe2 = s.op("tensor", lambda q, g=g: q.matmul(
                            s.PLp[0:1, :], s.ONECBF[:, :],
                            s.SQB[:, g * 512:(g + 1) * 512],
                            start=(g == 0), stop=(g == 1)),
                            waits=[("ve", vsq), ("ve", self.pl_free)],
                            inc="pe" if g == 1 else None)
                    sq_pe = pe2
                    rv = s.op("vector", lambda q: q.tensor_reduce(
                        s.NRM[0:1, 343:344], s.PLp[0:1, :], AX.X, ALU.max),
                        waits=[("pe", pe2)], inc="ve")
                    rqm_ve = s.op("vector", lambda q: q.tensor_tensor(
                        s.NRM[0:1, 34:35], s.NRM[0:1, 34:35],
                        s.NRM[0:1, 343:344], ALU.max),
                        waits=[("ve", rv), ("ve", rqm_ve)], inc="ve")
                    self.pl_free = rqm_ve
        bc = s.bcol("b_uq", s.wa["b_uq"], DC, scale=SA_A)
        st_uq = s.wstream_straight(s.wsa["w_uq"], 12, 512, 4, [s.ca["w_uq"]])

        def epi_q(m, nn2, slp, pe, bc=bc):
            stg = m % 2
            fw = [self.stg_free[stg]] if self.stg_free[stg][0] else []
            se = s.op("scalar", lambda q, nn2=nn2, slp=slp, stg=stg, m=m:
                      q.activation(
                          s.STG[:, stg * 1024 + nn2 * 512: stg * 1024 + (nn2 + 1) * 512],
                          s.PSLOT[slp], AF.Identity, bias=s.BCOL[:, m:m + 1],
                          scale=SA_A),
                      waits=[("pe", pe), ("ve", bc)] + fw, inc="se")
            if nn2 == 1:
                vq = s.op("vector", lambda q, stg=stg: q.scalar_tensor_tensor(
                    s.SQB[:, 0:1024], s.STG[:, stg * 1024:(stg + 1) * 1024], 1.0,
                    s.STG[:, stg * 1024:(stg + 1) * 1024], ALU.mult, ALU.mult),
                    waits=[("se", se)], inc="ve")
                for g in range(2):
                    pe2 = s.op("tensor", lambda q, g=g: q.matmul(
                        s.PLp[0:1, :], s.ONECBF[:, :],
                        s.SQB[:, g * 512:(g + 1) * 512], start=True, stop=True),
                        waits=[("ve", vq), ("ve", self.pl_free)], inc="pe")
                    self.pl_free = s.op("vector", lambda q, g=g, m=m:
                                        q.tensor_reduce(
                                            s.NRM[0:1, 64 + g * 16 + m:
                                                  65 + g * 16 + m],
                                            s.PLp[0:1, :], AX.X, ALU.max),
                                        waits=[("pe", pe2)], inc="ve")
                hs = s.dma("sync", s.qsp[m * 128:(m + 1) * 128, :],
                           s.STG[:, stg * 1024:(stg + 1) * 1024],
                           waits=[("se", se)], inc="hsC%d" % stg)
                self.stg_free[stg] = ("hsC%d" % stg, hs)
            return se

        s.proj_feat(12, st_uq, 512, 4, lambda k, nn2: lqt(k, nn2 * 512, 512),
                    epi_q, init_free=sfree)
        q_done_pe = s.n("pe")
        qsp_w = [("hsC0", s.n("hsC0")), ("hsC1", s.n("hsC1"))]
        qn_ve = s.n("ve")

        ngm = self.emit_negm(
            lambda r: s.gkm_a[r * (RD + 1) + RD: r * (RD + 1) + RD + 1, 0:32],
            True, [("cg", ccn)], qn_ve)
        st_o = s.wstream_straight(s.wsa["w_o"], DC, 512, 4, [s.ca["w_o"]])
        orow, osem, orow_gw = s.load_row("b_o", s.wa["b_o"], D)
        att_ve = self.emit_attention(True, s.gk_a, s.gv_a, kw, vw,
                                     [("cg", ccr)], ngm, q_done_pe, qsp_w)
        rve = self.emit_out_proj(
            lambda k, t: s.AT[:, k * 1024 + t * 128: k * 1024 + (t + 1) * 128],
            st_o, orow, [(osem, orow_gw)], extra_w=[("ve", att_ve)],
            ln_next=True)
        s.row_done("b_o", "pe", s.n("pe"))
        return rve, s.n("pe")

    def emit_ffn(self, r_ve, sa_free_pe, ln_next=False):
        s = self
        if self.pending_xln is not None:
            xln_ve, self.pending_xln = self.pending_xln, None
        else:
            xln_ve = s.emit_ln(r_ve, sa_free_pe)
        srow, ssem, sgw = s.load_row("f_s", s.wf["s"], D)

        def post_scale(reg, si, ld):
            pe = s.op("tensor", lambda q, si=si: q.matmul(
                s.PXp[:, 0:512], s.ONERBF[:, :], srow(si * 512, 512),
                start=True, stop=True),
                waits=[("hs", self.init_hs), (ssem, sgw),
                       ("ve", self.px_free), ("ve", self.sslb_free)], inc="pe")
            cp = s.op("vector", lambda q: q.tensor_copy(
                s.SSLB[:, :], s.PXp[:, 0:512]), waits=[("pe", pe)], inc="ve")
            self.px_free = cp
            ve = None
            for k in range(DC):
                ve = s.op("vector", lambda q, reg=reg, k=k: q.tensor_tensor(
                    s.WB[:, reg + k * 512: reg + (k + 1) * 512],
                    s.WB[:, reg + k * 512: reg + (k + 1) * 512],
                    s.SSLB[:, :], ALU.mult),
                    waits=[(ld[0], ld[1]), ("ve", cp)], inc="ve")
            self.sslb_free = ve
            return ("ve", ve)

        self.sslb_free = s.n("ve")
        st = s.wstream_T(s.wsc["f"], 512, 4, self.pc["f"], post=post_scale)
        frow, fsem, fgw = s.load_row("f_b", s.wf["b"], D)
        rve = self.emit_out_proj(
            lambda k, t: self.xlnt(k, t * 128, 128), st, frow,
            [(fsem, fgw)], extra_w=[("ve", xln_ve)], ln_next=ln_next)
        s.row_done("f_b", "pe", s.n("pe"))
        s.row_done("f_s", "pe", s.n("pe"))
        return rve, s.n("pe")

    def emit_attn_b(self, r_ve, sa_free_pe):
        s = self
        if self.pending_xln is not None:
            xln_ve, self.pending_xln = self.pending_xln, None
        else:
            xln_ve = s.emit_ln(r_ve, sa_free_pe)

        bc = s.bcol("bk", s.wbq["bk"], DC, scale=SA_B)
        st_k = s.wstream_T(s.wsc["k"], 512, 4, self.pc["k"])
        cwk = {}

        def epi_k(m, nn2, slp, pe, bc=bc):
            stg = m % 2
            fw = [self.stg_free[stg]] if self.stg_free[stg][0] else []
            se = s.op("scalar", lambda q, nn2=nn2, slp=slp, stg=stg, m=m:
                      q.activation(
                          s.STG[:, stg * 1024 + nn2 * 512: stg * 1024 + (nn2 + 1) * 512],
                          s.PSLOT[slp], AF.Identity, bias=s.BCOL[:, m:m + 1],
                          scale=SA_B),
                      waits=[("pe", pe), ("ve", bc)] + fw, inc="se")
            if nn2 == 1:
                vq = s.op("vector", lambda q, stg=stg: q.scalar_tensor_tensor(
                    s.SQB[:, 0:1024], s.STG[:, stg * 1024:(stg + 1) * 1024], 1.0,
                    s.STG[:, stg * 1024:(stg + 1) * 1024], ALU.mult, ALU.mult),
                    waits=[("se", se)], inc="ve")
                self.head_norm(lambda g: s.SQB[:, g * 512:(g + 1) * 512], 17 + m,
                               extra_w=[("ve", vq)])
                hs = s.dma("sync", s.ck_b[m // 8][(m % 8) * 128:
                                                  (m % 8 + 1) * 128, :],
                           s.STG[:, stg * 1024:(stg + 1) * 1024],
                           waits=[("se", se)], inc="hsC%d" % stg)
                self.stg_free[stg] = ("hsC%d" % stg, hs)
                if m % 8 == 7:
                    cwk[m // 8] = [("hsC0", s.n("hsC0")),
                                   ("hsC1", s.n("hsC1"))]
            return se

        s.proj_feat(DC, st_k, 512, 4, lambda k, nn2: self.xlnt(k, nn2 * 512, 512),
                    epi_k, extra_w=[("ve", xln_ve)])
        nv = s.op("vector", lambda q: q.tensor_copy(
            s.KROW[0:1, 0:16], s.NRM[0:1, 17:33]),
            waits=[("ve", s.n("ve"))], inc="ve")
        s.dma("sync", s.ckn_b[0:1, :], s.KROW[0:1, :],
              waits=[("ve", nv)], inc="hsK")
        cw_n = [("hsK", s.n("hsK"))]
        cck = {}
        for g in range(2):
            cck[g] = s.op("gpsimd", lambda q, g=g: q.collective_compute(
                "AllGather", ALU.bypass, replica_groups=GROUPS,
                ins=[s.ck_b[g]], outs=[s.gk_b[g]]), waits=cwk[g], inc="cg")
        ccn = s.op("gpsimd", lambda q: q.collective_compute(
            "AllGather", ALU.bypass, replica_groups=GROUPS,
            ins=[s.ckn_b], outs=[s.gkn_b]), waits=cw_n, inc="cg")

        st_v = s.wstream_T(s.wsc["v"], 512, 4, self.pc["v"])
        bvrow, bvsem, bvrow_gw = s.load_row("bv", s.wbq["bv"], D)
        cwv = {}

        def epi_v(si, t, slp, pe):
            stg = t % 2
            fw = [self.stg_free[stg]] if self.stg_free[stg][0] else []
            se = s.op("scalar", lambda q, slp=slp, stg=stg: q.activation(
                s.STG[:, stg * 1024: stg * 1024 + 512], s.PSLOT[slp], AF.Identity),
                waits=[("pe", pe)] + fw, inc="se")
            hs = s.dma("sync",
                       s.cv_b[si // 2][(si % 2) * 512:(si % 2 + 1) * 512,
                                       t * 128:(t + 1) * 128]
                       .rearrange("(h p) d -> p h d", p=128),
                       s.STG[:, stg * 1024: stg * 1024 + 512]
                       .rearrange("p (h d) -> p h d", d=128),
                       waits=[("se", se)], inc="hsC%d" % stg)
            self.stg_free[stg] = ("hsC%d" % stg, hs)
            if t == NT - 1 and si % 2 == 1:
                cwv[si // 2] = [("hsC0", s.n("hsC0")), ("hsC1", s.n("hsC1"))]
            return ("se", se)

        s.proj_tok(DC, st_v, 512, 4, lambda k, t: self.xlnt(k, t * 128, 128),
                   bvrow, [(bvsem, bvrow_gw)], epi_v)
        s.row_done("bv", "pe", s.n("pe"))
        ccv = {}
        for g in range(2):
            ccv[g] = s.op("gpsimd", lambda q, g=g: q.collective_compute(
                "AllGather", ALU.bypass, replica_groups=GROUPS,
                ins=[s.cv_b[g]], outs=[s.gv_b[g]]), waits=cwv[g], inc="cg")
        kw = [("cg", cck[g // 2]) for g in range(4)]
        vw = [("cg", ccv[g // 2]) for g in range(4)]

        bc = s.bcol("bq", s.wbq["bq"], DC, scale=SA_B)
        st_q = s.wstream_T(s.wsc["q"], 512, 4, self.pc["q"])

        def epi_q(m, nn2, slp, pe, bc=bc):
            stg = m % 2
            fw = [self.stg_free[stg]] if self.stg_free[stg][0] else []
            se = s.op("scalar", lambda q, nn2=nn2, slp=slp, stg=stg, m=m:
                      q.activation(
                          s.STG[:, stg * 1024 + nn2 * 512: stg * 1024 + (nn2 + 1) * 512],
                          s.PSLOT[slp], AF.Identity, bias=s.BCOL[:, m:m + 1],
                          scale=SA_B),
                      waits=[("pe", pe), ("ve", bc)] + fw, inc="se")
            if nn2 == 1:
                vq = s.op("vector", lambda q, stg=stg: q.scalar_tensor_tensor(
                    s.SQB[:, 0:1024], s.STG[:, stg * 1024:(stg + 1) * 1024], 1.0,
                    s.STG[:, stg * 1024:(stg + 1) * 1024], ALU.mult, ALU.mult),
                    waits=[("se", se)], inc="ve")
                for g in range(2):
                    pe2 = s.op("tensor", lambda q, g=g: q.matmul(
                        s.PLp[0:1, :], s.ONECBF[:, :],
                        s.SQB[:, g * 512:(g + 1) * 512], start=True, stop=True),
                        waits=[("ve", vq), ("ve", self.pl_free)], inc="pe")
                    self.pl_free = s.op("vector", lambda q, g=g, m=m:
                                        q.tensor_reduce(
                                            s.NRM[0:1, 64 + g * 16 + m:
                                                  65 + g * 16 + m],
                                            s.PLp[0:1, :], AX.X, ALU.max),
                                        waits=[("pe", pe2)], inc="ve")
                hs = s.dma("sync", s.qsp[m * 128:(m + 1) * 128, :],
                           s.STG[:, stg * 1024:(stg + 1) * 1024],
                           waits=[("se", se)], inc="hsC%d" % stg)
                self.stg_free[stg] = ("hsC%d" % stg, hs)
            return se

        s.proj_feat(DC, st_q, 512, 4, lambda k, nn2: self.xlnt(k, nn2 * 512, 512),
                    epi_q)
        q_done_pe = s.n("pe")
        qsp_w = [("hsC0", s.n("hsC0")), ("hsC1", s.n("hsC1"))]
        qn_ve = s.n("ve")

        ngm = self.emit_negm(
            lambda r: s.gkn_b[r:r + 1, 0:32], False, [("cg", ccn)], qn_ve)
        st_o = s.wstream_T(s.wsc["o"], 512, 4, self.pc["o"])
        borow, bosem, borow_gw = s.load_row("bo", s.wbq["bo"], D)
        att_ve = self.emit_attention(False, s.gk_b, s.gv_b, kw, vw,
                                     [], ngm, q_done_pe, qsp_w)
        rve = self.emit_out_proj(
            lambda k, t: s.AT[:, k * 1024 + t * 128: k * 1024 + (t + 1) * 128],
            st_o, borow, [(bosem, borow_gw)], extra_w=[("ve", att_ve)],
            ln_next=True)
        s.row_done("bo", "pe", s.n("pe"))
        return rve, s.n("pe")

    def emit_store(self, r_ve):
        st = self.dma("sync",
                      self.out_d[0:512, :].rearrange("(t p) d -> p t d", p=128),
                      self.R[:, 0:4 * D].rearrange("p (t d) -> p t d", d=D),
                      waits=[("ve", r_ve)], inc="hs")
        st2 = self.dma("scalar",
                       self.out_d[512:1024, :].rearrange("(t p) d -> p t d", p=128),
                       self.R[:, 4 * D:8 * D].rearrange("p (t d) -> p t d", d=D),
                       waits=[("ve", r_ve)], inc="haO")
        self.op("sync", lambda q: q.nop(), waits=[("hs", st), ("haO", st2)])

    def build(self):
        self.emit_init()
        r_ve = self.n("ve")
        sa_pe = 0
        if self.nsub >= 1:
            r_ve, sa_pe = self.emit_attn_a(
                r_ve, sa_pe, tile_w=lambda t: [("hsX%d" % (t // 2), 16)])
        if self.nsub >= 2:
            r_ve, sa_pe = self.emit_ffn(r_ve, sa_pe, ln_next=(self.nsub >= 3))
        if self.nsub >= 3:
            r_ve, sa_pe = self.emit_attn_b(r_ve, sa_pe)
        if self.nsub >= 4:
            r_ve, sa_pe = self.emit_ffn(r_ve, sa_pe)
        self.emit_store(r_ve)
        with self.nc.Block() as block:
            for e in ENGINES:
                stream = self.streams[e]
                if not stream:
                    continue

                chain_sem = {"vector": "ve", "scalar": "se",
                             "gpsimd": "po"}.get(e)

                def make(stream=stream, chain_sem=chain_sem):
                    def body(q):
                        hi = {}
                        chain = 0
                        for fn, waits, inc, by in stream:
                            if chain_sem is not None and chain > 0:
                                if hi.get(chain_sem, -1) < chain:
                                    q.wait_ge(self.sems[chain_sem], chain)
                                    hi[chain_sem] = chain
                            for snm, v in waits:
                                if hi.get(snm, -1) < v:
                                    q.wait_ge(self.sems[snm], v)
                                    hi[snm] = v
                            ins = fn(q)
                            if inc is not None:
                                ins.then_inc(self.sems[inc], by)
                                if inc == chain_sem:
                                    chain += by
                    return body
                getattr(block, e)(make())
        return self.nc


_BUILD_CACHE = {}


def _get_nc(nsub=NSUB):
    if nsub not in _BUILD_CACHE:
        _BUILD_CACHE[nsub] = MK(nsub).build()
    return _BUILD_CACHE[nsub]


def _host_consts(w):
    pos = (4.0 * np.arange(T, dtype=np.float32) + np.float32(w))
    inv = np.exp(np.arange(0, DH, 2, dtype=np.float32)
                 * np.float32(-math.log(10000.0) / DH))[: RD // 2]
    ang = pos[:, None] * inv[None, :]
    cos = np.ascontiguousarray(np.tile(np.cos(ang).astype(np.float32).T, (4, 1)))
    sin = np.ascontiguousarray(np.tile(np.sin(ang).astype(np.float32).T, (4, 1)))
    ii = np.arange(128)
    masks = np.zeros((4 * 128, 128), dtype=np.float32)
    for r in range(4):
        m = (4 * ii[:, None] + r <= 4 * ii[None, :] + w)
        masks[r * 128:(r + 1) * 128, :] = m.astype(np.float32)
    return cos, sin, masks


_PERMQ = np.empty(H * RD, np.int64)
for _c in range(4):
    for _j in range(128):
        _h = 4 * _c + _j // 32
        _i2 = _j % 32
        _PERMQ[(2 * _c) * 128 + _j] = _h * RD + 2 * _i2
        _PERMQ[(2 * _c + 1) * 128 + _j] = _h * RD + 2 * _i2 + 1
_PERMK = np.concatenate([np.arange(0, RD, 2), np.arange(1, RD, 2)])


def kernel(**inputs):
    global _dbg
    nc = _get_nc(NSUB)
    x = np.asarray(inputs["L0_attn_a_x"])
    base = {}
    for k in ("w_dkv", "b_dkv", "w_dq", "b_dq", "w_rk", "b_rk", "w_rq", "b_rq",
              "w_uk", "b_uk", "w_uv", "b_uv", "w_uq", "b_uq", "w_o", "b_o"):
        v = np.asarray(inputs["L0_attn_a_" + k])
        base["a_" + k] = v.reshape(1, -1) if v.ndim == 1 else v
    for k, perm in (("w_rq", _PERMQ), ("b_rq", _PERMQ),
                    ("w_rk", _PERMK), ("b_rk", _PERMK)):
        base["a_" + k] = np.ascontiguousarray(base["a_" + k][..., perm])
    for k in ("Wq", "bq", "Wk", "bk", "Wv", "bv", "Wo", "bo"):
        v = np.asarray(inputs["L0_attn_b_" + k])
        base["b_" + k] = v.reshape(1, -1) if v.ndim == 1 else v
    base["f_w"] = np.asarray(inputs["L0_ffn_weight"])
    base["f_s"] = np.asarray(inputs["L0_ffn_scale_weight"]).reshape(1, -1)
    base["f_b"] = np.asarray(inputs["L0_ffn_bias"]).reshape(1, -1)

    in_maps = []
    for core in range(8):
        b, w = core // 4, core % 4
        cos, sin, masks = _host_consts(w)
        m = dict(base)
        m["x"] = np.ascontiguousarray(x[b, w::4, :])
        m["cos_tab"] = cos
        m["sin_tab"] = sin
        m["masks"] = masks
        in_maps.append(m)

    res = run_bass_kernel_spmd(nc, in_maps, core_ids=list(range(8)))
    out = np.empty((B, S, D), dtype=np.float32)
    for core in range(8):
        b, w = core // 4, core % 4
        out[b, w::4, :] = res.results[core]["out"]
    if DEBUG:
        kernel._dbg = res.results
    return out
